# version 23 (final)
# baseline (speedup 1.0000x reference)
import contextlib
import numpy as np
import ml_dtypes
import concourse.bass as bass
import concourse.mybir as mybir
from concourse.bass_utils import run_bass_kernel_spmd

F32 = mybir.dt.float32
BF16 = mybir.dt.bfloat16
AF = mybir.ActivationFunctionType
ALU = mybir.AluOpType
bf16 = ml_dtypes.bfloat16

NCORES = 8
SEQS = 4
T = 2048
D = 1024
NCH = 8
CH = 512
NCHUNK = T // CH
HEADS = 16
M1 = 256
HID = 4096
CCH = 256
NKC = SEQS * T // CCH
EPS = 1e-5
RSQ_T = 1.0 / float(np.sqrt(np.float32(T)))



def prep_weights(i):
    w = {}

    def wchunks(a, nout):
        a = np.asarray(a, np.float32)
        return np.ascontiguousarray(
            a.reshape(8, 128, nout).transpose(1, 0, 2).reshape(128, 8 * nout)).astype(bf16)

    for nm, key in (("wq", "L0_attn_Wq"), ("wk", "L0_attn_Wk"),
                    ("wv", "L0_attn_Wv"), ("wg", "L0_attn_Wg"), ("wo", "L0_attn_Wo")):
        w[nm] = wchunks(i[key], 1024)

    def bcol(b, n=8):
        return np.ascontiguousarray(np.asarray(b, np.float32).reshape(n, 128).T)

    w["bq_c"] = bcol(i["L0_attn_bq"])
    w["bg_c"] = bcol(i["L0_attn_bg"])
    w["bo_c"] = bcol(i["L0_attn_bo"])
    w["bffn_c"] = bcol(np.concatenate([np.asarray(i["L0_ffn_bias_real"], np.float32),
                                       np.asarray(i["L0_ffn_bias_imag"], np.float32)]))
    w["bk_r"] = np.asarray(i["L0_attn_bk"], np.float32).reshape(1, 1024).astype(bf16)
    w["bv_r"] = np.asarray(i["L0_attn_bv"], np.float32).reshape(1, 1024).astype(bf16)

    def g1m(g1):
        a = np.asarray(g1, np.float32)[0]
        return np.ascontiguousarray(a.transpose(1, 0, 2).reshape(4, 1024))

    def g2m(g2):
        a = np.asarray(g2, np.float32)[:, :, 0, :]
        return np.ascontiguousarray(a.transpose(0, 2, 1).reshape(4, 256))

    w["g1r"] = g1m(i["L0_ffn_G1_R"]); w["g2r"] = g2m(i["L0_ffn_G2_R"])
    w["g1i"] = g1m(i["L0_ffn_G1_I"]); w["g2i"] = g2m(i["L0_ffn_G2_I"])

    w["l1_wk"] = wchunks(np.asarray(i["L1_attn_Wk"], np.float32).T, 256)
    w["l1_wv"] = wchunks(np.asarray(i["L1_attn_Wv"], np.float32).T, 256)
    w["l1_wq"] = wchunks(np.asarray(i["L1_attn_Wq"], np.float32).T, 256)
    w["l1_wg"] = wchunks(np.asarray(i["L1_attn_Wg"], np.float32).T, 1)
    w["l1_bg"] = np.asarray(i["L1_attn_bg"], np.float32).reshape(1, 1)
    w["l1_wo"] = wchunks(np.asarray(i["L1_attn_Wo"], np.float32).T.reshape(2, 128, 1024)
                         .reshape(256, 1024), 1024)[:, :2048] if False else None
    a = np.asarray(i["L1_attn_Wo"], np.float32).T
    w["l1_wo"] = np.ascontiguousarray(
        a.reshape(2, 128, 1024).transpose(1, 0, 2).reshape(128, 2048)).astype(bf16)
    w["l1_wfc"] = wchunks(np.asarray(i["L1_ffn_W_fc"], np.float32).T, 4096)
    a = np.asarray(i["L1_ffn_W_proj"], np.float32).T
    w["l1_wproj"] = np.ascontiguousarray(
        a.reshape(32, 128, 1024).transpose(1, 0, 2).reshape(128, 32768)).astype(bf16)
    return w



class Sem:
    __slots__ = ("h", "n", "name")

    def __init__(self, h, name):
        self.h = h
        self.n = 0
        self.name = name

    def inc(self, k=1):
        self.n += k
        return (self, k)


class K:
    ENGINES = ("sync", "tensor", "vector", "scalar", "gpsimd")

    def __init__(self, nc):
        self.nc = nc
        self.ops = {e: [] for e in self.ENGINES}
        self._lastwait = {e: {} for e in self.ENGINES}

    def sem(self, ctx, name):
        return Sem(ctx.enter_context(self.nc.semaphore(name)), name)

    def wait(self, eng, sem, val):
        if sem is None or val <= 0:
            return
        lw = self._lastwait[eng]
        if lw.get(sem, -1) >= val:
            return
        lw[sem] = val
        self.ops[eng].append(("w", sem, val))

    def op(self, eng, build, waits=(), inc=None):
        for (s, v) in waits:
            self.wait(eng, s, v)
        incs = []
        if inc is not None:
            incs = [inc] if isinstance(inc, tuple) else [x for x in inc if x]
        self.ops[eng].append(("o", build, incs))

    def dma(self, queue, out, in_, waits=(), inc=None):
        self.op(queue, lambda e, o=out, i=in_: e.dma_start(out=o, in_=i), waits, inc)

    def mm(self, out, lhsT, rhs, start, stop, waits=(), inc=None):
        self.op("tensor",
                lambda e, o=out, l=lhsT, r=rhs, st=start, sp=stop:
                e.matmul(o, l, r, start=st, stop=sp),
                waits, inc)

    def emit(self, ctx):
        block = ctx.enter_context(self.nc.Block())
        for eng_name in self.ENGINES:
            items = self.ops[eng_name]
            if not items:
                continue

            def body(eng, items=items):
                for it in items:
                    if it[0] == "w":
                        eng.wait_ge(it[1].h, it[2])
                    else:
                        ins = it[1](eng)
                        for (s, kk) in it[2]:
                            ins = ins.then_inc(s.h, kk)
            getattr(block, eng_name)(body)



def build_program(debug=(), phases=("wbuild", "l0a", "l0b", "l1a", "l1b", "l1c")):
    nc = bass.Bass("TRN2", target_bir_lowering=False)
    ctx = contextlib.ExitStack()

    def kind(name):
        return "ExternalOutput" if name in debug else "Internal"

    x_d = nc.dram_tensor("x", [SEQS, D, T], F32, kind="ExternalInput")
    y_d = nc.dram_tensor("y", [SEQS, D, T], F32, kind="ExternalOutput")
    x1_d = nc.dram_tensor("x1", [SEQS, D, T], F32, kind=kind("x1"))
    xo_d = nc.dram_tensor("xo", [NKC, 128, 2048], F32, kind=kind("xo"))
    xn0_d = nc.dram_tensor("xn0", [SEQS, NCHUNK, 128, 4096], BF16, kind=kind("xn0"))
    xn1_d = nc.dram_tensor("xn1", [NKC, 128, 2048], BF16, kind=kind("xn1"))
    keys_d = nc.dram_tensor("keysd", [SEQS, 16, 128, M1], BF16, kind=kind("keysd"))
    glog_d = nc.dram_tensor("glogd", [SEQS, T], F32, kind=kind("glogd"))
    cum_d = nc.dram_tensor("cumd", [SEQS, T], F32, kind=kind("cumd"))
    t1r_d = nc.dram_tensor("t1r", [D, 256], BF16, kind=kind("t1r"))
    t1i_d = nc.dram_tensor("t1i", [D, 256], BF16, kind=kind("t1i"))
    kvdbg_d = (nc.dram_tensor("kvdbg", [SEQS, 128, 512], F32, kind="ExternalOutput")
               if "kvdbg" in debug else None)
    cumdbg_d = (nc.dram_tensor("cumdbg", [128, 177], F32, kind="ExternalOutput")
                if "cumdbg" in debug else None)

    def din(name, shape, dt):
        return nc.dram_tensor(name, shape, dt, kind="ExternalInput")

    wq_i = din("wq", [128, 8192], BF16); wk_i = din("wk", [128, 8192], BF16)
    wv_i = din("wv", [128, 8192], BF16); wg_i = din("wg", [128, 8192], BF16)
    wo_i = din("wo", [128, 8192], BF16)
    bq_i = din("bq_c", [128, 8], F32); bg_i = din("bg_c", [128, 8], F32)
    bo_i = din("bo_c", [128, 8], F32); bffn_i = din("bffn_c", [128, 8], F32)
    bk_i = din("bk_r", [1, 1024], BF16); bv_i = din("bv_r", [1, 1024], BF16)
    g1r_i = din("g1r", [4, 1024], F32); g2r_i = din("g2r", [4, 256], F32)
    g1i_i = din("g1i", [4, 1024], F32); g2i_i = din("g2i", [4, 256], F32)
    l1wk_i = din("l1_wk", [128, 2048], BF16); l1wv_i = din("l1_wv", [128, 2048], BF16)
    l1wq_i = din("l1_wq", [128, 2048], BF16); l1wg_i = din("l1_wg", [128, 8], BF16)
    l1bg_i = din("l1_bg", [1, 1], F32)
    l1wo_i = din("l1_wo", [128, 2048], BF16)
    l1wfc_i = din("l1_wfc", [128, 32768], BF16)
    l1wproj_i = din("l1_wproj", [128, 32768], BF16)

    AW_BYTES = 131072
    AA_BYTES = 61440
    aw_t = ctx.enter_context(nc.sbuf_tensor("arena_w", [128, AW_BYTES], mybir.dt.uint8))
    aa_t = ctx.enter_context(nc.sbuf_tensor("arena_a", [128, AA_BYTES], mybir.dt.uint8))
    aw_base = nc.lookup_mloc(aw_t).addr
    aa_base = nc.lookup_mloc(aa_t).addr

    def at_w(name, shape, dt, off):
        per = int(np.prod(shape[1:])) * mybir.dt.size(dt)
        assert off + per <= AW_BYTES, (name, off, per)
        return nc.alloc_sbuf_tensor_at(name, shape, dt, offset=aw_base + off)

    def at_a(name, shape, dt, off):
        per = int(np.prod(shape[1:])) * mybir.dt.size(dt)
        assert off + per <= AA_BYTES, (name, off, per)
        return nc.alloc_sbuf_tensor_at(name, shape, dt, offset=aa_base + off)

    wq = at_w("wq_s", [128, 8192], BF16, 0)
    wk = at_w("wk_s", [128, 8192], BF16, 16384)
    wv = at_w("wv_s", [128, 8192], BF16, 32768)
    wg = at_w("wg_s", [128, 8192], BF16, 49152)
    wo = at_w("wo_s", [128, 8192], BF16, 65536)
    wrT = at_w("wrT", [128, 2048], BF16, 81920)
    wiT = at_w("wiT", [128, 2048], BF16, 86016)
    wiTn = at_w("wiTn", [128, 2048], BF16, 90112)
    g1r = at_w("g1r_s", [4, 1024], F32, 94208)
    g2r = at_w("g2r_s", [4, 256], F32, 98304)
    g1i = at_w("g1i_s", [4, 1024], F32, 99328)
    g2i = at_w("g2i_s", [4, 256], F32, 103424)
    bkb = at_w("bkb", [128, 1024], BF16, 104448)
    bvb = at_w("bvb", [128, 1024], BF16, 106496)
    tmpw = at_w("tmpw", [128, 256], BF16, 108544)
    bk_r = at_w("bk_rs", [1, 1024], BF16, 109056)
    bv_r = at_w("bv_rs", [1, 1024], BF16, 111104)
    wfc = at_w("wfc_s", [128, 32768], BF16, 0)
    wproj = at_w("wproj_s", [128, 32768], BF16, 65536)
    l1wk = at_w("l1wk_s", [128, 2048], BF16, 65536)
    l1wv = at_w("l1wv_s", [128, 2048], BF16, 69632)
    l1wq = at_w("l1wq_s", [128, 2048], BF16, 73728)
    l1wo = at_w("l1wo_s", [128, 2048], BF16, 77824)
    def sbst(name, shape, dt):
        return ctx.enter_context(nc.sbuf_tensor(name, shape, dt))

    l1wg = sbst("l1wg_s", [128, 8], BF16)

    l1bg = sbst("l1bg_s", [1, 1], F32)
    bq_c = sbst("bq_cs", [128, 8], F32)
    bg_c = sbst("bg_cs", [128, 8], F32)
    bo_c = sbst("bo_cs", [128, 8], F32)
    bffn_c = sbst("bffn_cs", [128, 8], F32)
    ones_bf = sbst("ones_bf", [1, 128], BF16)
    ones_f32 = sbst("ones_f32", [1, 128], F32)
    ones_col = sbst("ones_col", [128, 1], BF16)
    U_tri = sbst("U_tri", [128, 128], F32)
    kv_acc = sbst("kv_acc", [128, 512], F32)
    kv_bf = sbst("kv_bf", [128, 512], BF16)
    m_rowc = sbst("m_rowc", [1, 1024], BF16)
    rstd_rowc = sbst("rstd_rowc", [1, 1024], F32)
    glog_rowc = sbst("glog_rowc", [1, 1024], F32)
    cumrow = sbst("cumrow", [1, 512], F32)
    ln_row = sbst("ln_row", [1, 512], F32)
    prefcol = sbst("prefcol", [128, 1], F32)
    pj_a = sbst("pj_a", [128, 16], F32)
    pj_b = sbst("pj_b", [128, 16], F32)
    cumtm = sbst("cumtm", [128, 16], F32)
    zeros16 = sbst("zeros16", [128, 16], F32)
    c_eps = sbst("c_eps", [128, 1], F32)
    c_eps8 = sbst("c_eps8", [128, 1], F32)
    nc.const_aps.register(F32, EPS, c_eps[:, :])
    nc.const_aps.register(F32, 1e-8, c_eps8[:, :])

    xhalf = at_a("xhalf", [128, 2048], F32, 0)
    xb = at_a("xb", [128, 4096], BF16, 8192)
    xnc = at_a("xnc", [128, 4096], BF16, 16384)
    mb_s = at_a("mb_s", [128, 512], BF16, 24576)
    xsq = at_a("xsq", [128, 4096], BF16, 47104)
    xsq_st = at_a("xsq_st", [128, 2048], F32, 47104)
    rb_s = at_a("rb_s", [128, 512], F32, 55296)
    u_s = [at_a(f"u_s{j}", [128, 512], F32, 25600 + 2048 * j) for j in range(2)]
    m1_s = at_a("m1_s", [128, 512], F32, 29696)
    e_s = at_a("e_s", [128, 512], F32, 31744)
    r_s = at_a("r_s", [128, 512], F32, 33792)
    kP = [at_a(f"kP{j}", [128, 1024], BF16, 35840 + 2048 * j) for j in range(2)]
    vP = [at_a(f"vP{j}", [128, 1024], BF16, 39936 + 2048 * j) for j in range(2)]
    valsb = at_a("valsb", [128, 4096], BF16, 25600)
    quersb = at_a("quersb", [128, 4096], BF16, 33792)
    keyt = [at_a(f"keyt{j}", [128, 256], BF16, 44032 + 512 * j) for j in range(2)]
    xncB = at_a("xncB", [128, 4096], BF16, 0)
    xslB = [at_a(f"xslB{j}", [128, 512], F32, 8192 + 2048 * j) for j in range(3)]
    qP = at_a("qP", [128, 4096], BF16, 14336)
    gateB = [at_a(f"gateB{j}", [128, 512], F32, 22528 + 2048 * j) for j in range(4)]
    gaB = at_a("gaB", [128, 4096], BF16, 30720)
    m1B = [at_a(f"m1B{j}", [128, 512], F32, 38912 + 2048 * j) for j in range(2)]
    eB = [at_a(f"eB{j}", [128, 512], F32, 43008 + 2048 * j) for j in range(2)]
    rB_ = [at_a(f"rB_{j}", [128, 512], F32, 47104 + 2048 * j) for j in range(2)]
    cumjb = at_a("cumjb", [128, 2048], F32, 0)
    x1cB1 = at_a("x1cB1", [128, 4096], F32, 8192)
    keytB = [at_a(f"keytB{j}", [128, 256], BF16, 24576 + 512 * j) for j in range(2)]
    dscr = [at_a(f"dscr{j}", [128, 512], F32, 41984 + 2048 * j) for j in range(4)]
    dscr2 = [at_a(f"dscr2_{j}", [128, 1024], F32, 41984 + 4096 * j) for j in range(2)]
    wtb = [at_a(f"wtb{j}", [128, 512], BF16, 50176 + 1024 * j) for j in range(4)]
    readsb = at_a("readsb", [128, 1024], BF16, 54272)
    xnC = [at_a(f"xnC{j}", [128, 2048], BF16, 4096 * j) for j in range(2)]
    hC = at_a("hC", [128, 8192], BF16, 8192)
    yC = at_a("yC", [128, 2048], F32, 24576)
    memC = at_a("memC", [128, 2048], F32, 32768)
    spkC = at_a("spkC", [128, 2048], BF16, 40960)
    accC = at_a("accC", [128, 2048], BF16, 45056)
    xoC = at_a("xoC", [128, 2048], F32, 49152)
    rrC = [at_a(f"rrC{j}", [128, 512], F32, 57344 + 2048 * j) for j in range(2)]

    pbk = [ctx.enter_context(nc.psum_tensor(f"pb{j}", [128, 512], F32)) for j in range(8)]

    k = K(nc)
    s_pe = k.sem(ctx, "s_pe")
    s_dve = k.sem(ctx, "s_dve")
    s_act = k.sem(ctx, "s_act")
    s_gp = k.sem(ctx, "s_gp")
    s_wl = k.sem(ctx, "s_wl")
    s_xsl = [k.sem(ctx, f"s_xsl{j}") for j in range(2)]
    s_xcp = k.sem(ctx, "s_xcp")
    s_xnl = k.sem(ctx, "s_xnl")
    s_xslB = [k.sem(ctx, f"s_xslB{j}") for j in range(3)]
    s_stxn = k.sem(ctx, "s_stxn")
    s_stx1 = [k.sem(ctx, f"s_stx1{j}") for j in range(3)]
    s_stkey = [k.sem(ctx, f"s_stkey{j}") for j in range(2)]
    s_stgl = k.sem(ctx, "s_stgl")
    s_strow = k.sem(ctx, "s_strow")
    s_stt1 = k.sem(ctx, "s_stt1")
    s_keyl = [k.sem(ctx, f"s_keyl{j}") for j in range(2)]
    s_x1l = [k.sem(ctx, f"s_x1l{j}") for j in range(4)]
    s_rowl = k.sem(ctx, "s_rowl")
    s_wfc = k.sem(ctx, "s_wfc")
    s_xnlC = k.sem(ctx, "s_xnlC")
    s_xol = k.sem(ctx, "s_xol")
    s_sty = k.sem(ctx, "s_sty")
    s_stxo = [k.sem(ctx, f"s_stxo{j}") for j in range(2)]

    def fm_ap(dram, s, t0, tlen):
        return dram[s, :, t0:t0 + tlen].rearrange("(ic p) t -> p ic t", p=128)

    for (dst, src) in ((wq, wq_i), (wk, wk_i), (wv, wv_i), (wg, wg_i), (wo, wo_i),
                       (g1r, g1r_i), (g2r, g2r_i), (g1i, g1i_i), (g2i, g2i_i),
                       (bk_r, bk_i), (bv_r, bv_i), (l1bg, l1bg_i),
                       (bq_c, bq_i), (bg_c, bg_i), (bo_c, bo_i), (bffn_c, bffn_i)):
        k.dma("sync", dst[:], src[:], inc=s_wl.inc(16))
    WL0 = s_wl.n

    for t, v in ((ones_bf, 1.0), (ones_f32, 1.0), (ones_col, 1.0), (U_tri, 1.0),
                 (zeros16, 0.0), (c_eps, EPS), (c_eps8, 1e-8)):
        k.op("gpsimd", lambda e, t=t, v=v: e.memset(t[:], v))
    k.op("gpsimd", lambda e: e.affine_select(U_tri[:], U_tri[:], [[1, 128]], ALU.is_gt,
                                             0.0, base=0, channel_multiplier=-1),
         inc=s_gp.inc())
    GP_CONST = s_gp.n

    WBUILD_DVE = 0
    if "wbuild" in phases and "skip_bcast" not in debug:
        for half in range(2):
            k.mm(pbk[6][:, :], ones_bf[:], bk_r[0:1, half * 512:(half + 1) * 512],
                 True, True, waits=[(s_wl, WL0), (s_gp, GP_CONST), (s_dve, s_dve.n)])
            k.mm(pbk[7][:, :], ones_bf[:], bv_r[0:1, half * 512:(half + 1) * 512],
                 True, True, inc=s_pe.inc())
            pe_n = s_pe.n
            k.op("vector", lambda e, h=half: e.tensor_copy(bkb[:, h * 512:(h + 1) * 512], pbk[6][:]),
                 waits=[(s_pe, pe_n)])
            k.op("vector", lambda e, h=half: e.tensor_copy(bvb[:, h * 512:(h + 1) * 512], pbk[7][:]),
                 inc=s_dve.inc())
    if "wbuild" in phases and "skip_t1" not in debug:
        for (g1_s, g2_s, t1_t) in ((g1r, g2r, t1r_d), (g1i, g2i, t1i_d)):
            for oc in range(8):
                k.mm(pbk[5][:, 0:256], g1_s[:, oc * 128:(oc + 1) * 128], g2_s[:],
                     True, True,
                     waits=[(s_wl, WL0), (s_dve, s_dve.n), (s_stt1, s_stt1.n)],
                     inc=s_pe.inc())
                pe_n = s_pe.n
                k.op("vector", lambda e: e.tensor_copy(tmpw[:], pbk[5][:, 0:256]),
                     waits=[(s_pe, pe_n)], inc=s_dve.inc())
                k.dma("scalar", t1_t[oc * 128:(oc + 1) * 128, :], tmpw[:],
                      waits=[(s_dve, s_dve.n)], inc=s_stt1.inc(16))
        T1_DONE = s_stt1.n
        for (t1_t, w_t) in ((t1r_d, wrT), (t1i_d, wiT)):
            rr_ap = t1_t.ap().rearrange("(b d) (g e) -> d g b e", b=32, d=32, g=16, e=16)
            for ic in range(4):
                for oc in range(4):
                    k.dma("sync", w_t[:, (ic * 4 + oc) * 128:(ic * 4 + oc + 1) * 128],
                          rr_ap[ic * 8:(ic + 1) * 8, :, oc * 8:(oc + 1) * 8, :],
                          waits=[(s_stt1, T1_DONE)], inc=s_wl.inc(16))
        WT_DONE = s_wl.n
        k.op("vector", lambda e: e.tensor_scalar_mul(wiTn[:], wiT[:], -1.0),
             waits=[(s_wl, WT_DONE)], inc=s_dve.inc())
        WBUILD_DVE = s_dve.n

    XB = [xb, xnc]
    ln_state = {"cast_war": 0, "stats_pe": 0, "pb0_act": 0, "pb0_dve": 0,
                "pb1_act": 0, "pb2_dve": 0, "pb3_dve": 0,
                "mrow_pe": 0, "rrow_pe": 0, "xbuf_war": [0, 0], "sq_pe": 0,
                "kv_in": [0, 0], "kP_war": [0, 0]}

    def ln_front(src_d, s, c, first_waits=()):
        X = XB[c % 2]
        war0 = ln_state["cast_war"]
        for h in range(2):
            stg = xhalf if h == 0 else xsq_st
            lw = [(s_xcp, war0)] + list(first_waits)
            if h == 1:
                lw.append((s_pe, ln_state["sq_pe"]))
            k.dma("sync", stg[:],
                  src_d[s, h * 512:(h + 1) * 512, c * CH:(c + 1) * CH]
                  .rearrange("(ic p) t -> p ic t", p=128),
                  waits=lw, inc=s_xsl[0].inc(16))
            k.op("vector",
                 lambda e, h=h, X=X, stg=stg:
                 e.tensor_copy(X[:, h * 2048:(h + 1) * 2048], stg[:]),
                 waits=[(s_xsl[0], s_xsl[0].n), (s_pe, ln_state["xbuf_war"][c % 2])],
                 inc=s_xcp.inc())
        ln_state["cast_war"] = s_xcp.n
        cast_done = s_xcp.n
        for ic in range(NCH):
            k.mm(pbk[0][0:1, :], ones_col[:], X[:, ic * CH:(ic + 1) * CH],
                 ic == 0, ic == NCH - 1,
                 waits=([(s_xcp, cast_done), (s_act, ln_state["pb0_act"]),
                         (s_dve, ln_state["pb0_dve"])] if ic == 0 else ()))
        k.op("scalar",
             lambda e, X=X: e.activation(xsq[:], X[:], AF.Square, bias=0.0, scale=1.0),
             waits=[(s_xcp, cast_done), (s_pe, ln_state["sq_pe"])], inc=s_act.inc())
        sq_act = s_act.n
        for ic in range(NCH):
            k.mm(pbk[0][32:33, :], ones_col[:], xsq[:, ic * CH:(ic + 1) * CH],
                 ic == 0, ic == NCH - 1,
                 waits=([(s_act, sq_act)] if ic == 0 else ()),
                 inc=(s_pe.inc() if ic == NCH - 1 else None))
        ln_state["stats_pe"] = s_pe.n
        ln_state["sq_pe"] = s_pe.n
        return s_pe.n

    def ln_back(xn_dst_d, s, c, stats_done):
        X = XB[c % 2]
        rot = c % 2
        k.op("scalar", lambda e: e.activation(ln_row[0:1, :], pbk[0][0:1, :], AF.Square,
                                              bias=0.0, scale=1.0 / 1024),
             waits=[(s_pe, stats_done)], inc=s_act.inc())
        k.op("vector", lambda e: e.scalar_tensor_tensor(pbk[1][0:1, 0:CH], pbk[0][32:33, :],
                                                        1.0 / 1024, ln_row[0:1, :],
                                                        ALU.mult, ALU.subtract),
             waits=[(s_act, s_act.n), (s_act, ln_state["pb1_act"]), (s_pe, stats_done)],
             inc=s_dve.inc())
        k.op("scalar", lambda e: e.activation(pbk[1][64:65, 0:CH], pbk[1][0:1, 0:CH], AF.Ln,
                                              bias=EPS, scale=1.0),
             waits=[(s_dve, s_dve.n)], inc=s_act.inc())
        k.op("scalar", lambda e, r=rot: e.activation(rstd_rowc[0:1, r * 512:(r + 1) * 512],
                                                     pbk[1][64:65, 0:CH], AF.Exp,
                                                     bias=0.0, scale=-0.5),
             waits=[(s_pe, ln_state["rrow_pe"])], inc=s_act.inc())
        rows_act = s_act.n
        ln_state["pb1_act"] = rows_act
        k.op("vector", lambda e, r=rot: e.tensor_scalar_mul(m_rowc[0:1, r * 512:(r + 1) * 512],
                                                            pbk[0][0:1, :], 1.0 / 1024),
             waits=[(s_pe, stats_done), (s_pe, ln_state["mrow_pe"])], inc=s_dve.inc())
        rows_dve = s_dve.n
        ln_state["pb0_act"] = rows_act
        ln_state["pb0_dve"] = rows_dve
        k.mm(pbk[2][:, :], ones_bf[:], m_rowc[0:1, rot * 512:(rot + 1) * 512], True, True,
             waits=[(s_dve, rows_dve), (s_dve, ln_state["pb2_dve"])])
        ln_state["mrow_pe"] = s_pe.n + 1
        k.mm(pbk[3][:, :], ones_f32[:], rstd_rowc[0:1, rot * 512:(rot + 1) * 512], True, True,
             waits=[(s_act, rows_act), (s_dve, ln_state["pb3_dve"])], inc=s_pe.inc())
        ln_state["rrow_pe"] = s_pe.n
        bc_pe = s_pe.n
        k.op("vector", lambda e: e.tensor_copy(mb_s[:], pbk[2][:]),
             waits=[(s_pe, bc_pe)], inc=s_dve.inc())
        ln_state["pb2_dve"] = s_dve.n
        k.op("vector", lambda e: e.tensor_copy(rb_s[:], pbk[3][:]), inc=s_dve.inc())
        ln_state["pb3_dve"] = s_dve.n
        mbrb_dve = s_dve.n
        X_v = X[:].rearrange("p (r t) -> p r t", r=NCH)
        mb_b = bass.AP(mb_s, 0, [[512, 128], [0, NCH], [1, CH]])
        rb_b = bass.AP(rb_s, 0, [[512, 128], [0, NCH], [1, CH]])
        k.op("vector", lambda e, X_v=X_v: e.tensor_tensor(X_v, X_v, mb_b, ALU.subtract),
             waits=[(s_dve, mbrb_dve)], inc=None)
        k.op("vector", lambda e, X_v=X_v: e.tensor_tensor(X_v, X_v, rb_b, ALU.mult),
             inc=s_dve.inc())
        xnc_dve = s_dve.n
        return xnc_dve

    def ln_store(xn_dst_d, s, c, xnc_dve):
        X = XB[c % 2]
        if xn_dst_d is xn0_d:
            k.dma("scalar", xn0_d[s, c], X[:], waits=[(s_dve, xnc_dve)],
                  inc=s_stxn.inc(16))
        elif xn_dst_d is xn1_d:
            xv = X[:].rearrange("p (ic t) -> p ic t", ic=NCH)
            for half in range(2):
                kc = s * 8 + c * 2 + half
                k.dma("scalar", xn1_d[kc].rearrange("p (ic t) -> p ic t", ic=NCH),
                      xv[:, :, half * 256:(half + 1) * 256],
                      waits=[(s_dve, xnc_dve)], inc=s_stxn.inc(16))

    kv_ready = {}
    l0_xn_store = {}
    bank_war = {j: (None, 0) for j in range(8)}

    def bw(j):
        sem, n = bank_war[j]
        return [(sem, n)] if sem else []

    l0b_pe_end = 0
    l0b_st_end = [0, 0, 0]
    if "l0a" in phases or "l0b" in phases:

        def heavy_l0a(s, c, xnc_dve):
            X = XB[c % 2]

            def kv_mms(tt):
                trot = tt % 2
                kv_in = ln_state["kv_in"][trot]
                tile_idx = c * (CH // 128) + tt
                bank = 6 + tile_idx % 2
                for h in range(HEADS):
                    pb_, fb_ = 64 * (h % 2), 64 * (h // 2)
                    k.mm(pbk[bank][pb_:pb_ + 64, fb_:fb_ + 64],
                         kP[trot][:, 64 * h: 64 * h + 64],
                         vP[trot][:, 64 * h: 64 * h + 64],
                         True, True,
                         waits=([(s_dve, kv_in)] + bw(bank)) if h == 0 else (),
                         inc=(s_pe.inc() if h == HEADS - 1 else None))
                pe_n = s_pe.n
                if tile_idx == 0:
                    k.op("vector", lambda e, b=bank: e.tensor_copy(kv_acc[:], pbk[b][:]),
                         waits=[(s_pe, pe_n), (s_pe, l0b_pe_end)], inc=s_dve.inc())
                else:
                    k.op("vector", lambda e, b=bank: e.tensor_add(kv_acc[:], kv_acc[:], pbk[b][:]),
                         waits=[(s_pe, pe_n)], inc=s_dve.inc())
                bank_war[bank] = (s_dve, s_dve.n)
                ln_state["kP_war"][trot] = s_pe.n

            for tt in range(CH // 128):
                trot = tt % 2
                for (wmat, dstP, bias_b, is_k) in ((wk, kP[trot], bkb, True),
                                                   (wv, vP[trot], bvb, False)):
                    for half in range(2):
                        bank = 4 + half
                        for ic in range(NCH):
                            k.mm(pbk[bank][:, :],
                                 X[:, ic * CH + tt * 128: ic * CH + tt * 128 + 128],
                                 wmat[:, ic * 1024 + half * 512: ic * 1024 + half * 512 + 512],
                                 ic == 0, ic == NCH - 1,
                                 waits=([(s_dve, xnc_dve),
                                         (s_pe, ln_state["kP_war"][trot])] + bw(bank))
                                       if ic == 0 else (),
                                 inc=(s_pe.inc() if ic == NCH - 1 else None))
                        pe_n = s_pe.n
                        if is_k:
                            k.op("vector", lambda e, b=bank, h=half:
                                 e.scalar_tensor_tensor(u_s[h][:], pbk[b][:], 1.0,
                                                        bkb[:, h * 512:(h + 1) * 512],
                                                        ALU.mult, ALU.add),
                                 waits=[(s_pe, pe_n),
                                        (s_act, ln_state.get(f"u{half}_war", 0))],
                                 inc=s_dve.inc())
                            bank_war[bank] = (s_dve, s_dve.n)
                            k.op("vector", lambda e, h=half: e.tensor_scalar_min(m1_s[:], u_s[h][:], 0.0),
                                 waits=[(s_act, ln_state.get("m1_war", 0))],
                                 inc=s_dve.inc())
                            m1_n = s_dve.n
                            k.op("scalar", lambda e: e.activation(e_s[:], m1_s[:], AF.Exp,
                                                                  bias=0.0, scale=1.0),
                                 waits=[(s_dve, m1_n), (s_dve, ln_state.get("e_war", 0))],
                                 inc=s_act.inc())
                            ln_state["m1_war"] = s_act.n
                            k.op("scalar", lambda e, h=half: e.activation(r_s[:], u_s[h][:], AF.Relu,
                                                                          bias=0.0, scale=1.0),
                                 waits=[(s_dve, ln_state.get("r_war", 0))], inc=s_act.inc())
                            ln_state[f"u{half}_war"] = s_act.n
                            act_n = s_act.n
                            k.op("vector", lambda e, d=dstP, h=half:
                                 e.tensor_add(d[:, h * 512:(h + 1) * 512], e_s[:], r_s[:]),
                                 waits=[(s_act, act_n)], inc=s_dve.inc())
                            ln_state["e_war"] = s_dve.n
                            ln_state["r_war"] = s_dve.n
                        else:
                            k.op("vector", lambda e, b=bank, d=dstP, h=half:
                                 e.scalar_tensor_tensor(d[:, h * 512:(h + 1) * 512], pbk[b][:],
                                                        1.0, bvb[:, h * 512:(h + 1) * 512],
                                                        ALU.mult, ALU.add),
                                 waits=[(s_pe, pe_n)], inc=s_dve.inc())
                            bank_war[bank] = (s_dve, s_dve.n)
                ln_state["kv_in"][trot] = s_dve.n
                if tt > 0:
                    kv_mms(tt - 1)
            kv_mms(CH // 128 - 1)
            ln_state["xbuf_war"][c % 2] = s_pe.n

        for s in range(SEQS):
            if s == 0:
                fw0 = [(s_wl, WL0), (s_dve, WBUILD_DVE), (s_gp, GP_CONST)]
            else:
                fw0 = ([(s_pe, l0b_pe_end)]
                       + [(s_stx1[j], l0b_st_end[j]) for j in range(3)])
            stats = {}
            xn_dve = {}
            for c in range(NCHUNK):
                stats[c] = ln_front(x_d, s, c, first_waits=(fw0 if c == 0 else ()))
                xn_dve[c] = ln_back(xn0_d, s, c, stats[c])
                if c >= 1:
                    heavy_l0a(s, c - 1, xn_dve[c - 1])
                ln_store(xn0_d, s, c, xn_dve[c])
            heavy_l0a(s, NCHUNK - 1, xn_dve[NCHUNK - 1])
            k.op("vector", lambda e: e.tensor_copy(kv_bf[:], kv_acc[:]),
                 waits=[(s_pe, l0b_pe_end)], inc=s_dve.inc())
            kv_ready[s] = s_dve.n
            l0_xn_store[s] = s_stxn.n
            if kvdbg_d is not None:
                k.dma("scalar", kvdbg_d[s], kv_acc[:], waits=[(s_dve, kv_ready[s])],
                      inc=s_strow.inc(16))
            if "l0b" not in phases:
                continue
            st_b = {"m1B_act": [0, 0], "eB_dve": [0, 0], "rB_dve": [0, 0],
                    "gate_dve": [0, 0, 0, 0], "ga_pe": 0,
                    "qP_pe": s_pe.n,
                    "xn_pe": 0}
            ring = [0]
            def rbank():
                b = ring[0] % 8
                ring[0] += 1
                return b
            q_dve_oc = [0] * 8
            for c in range(NCHUNK):
                k.dma("sync", xncB[:], xn0_d[s, c],
                      waits=[(s_stxn, l0_xn_store[s]), (s_pe, st_b["xn_pe"]),
                             (s_dve, kv_ready[s])],
                      inc=s_xnl.inc(16))
                xn_ld = s_xnl.n
                for oc in range(8):
                    bank = rbank()
                    r2 = oc % 2
                    for ic in range(NCH):
                        k.mm(pbk[bank][:, :], wq[:, ic * 1024 + oc * 128: ic * 1024 + oc * 128 + 128],
                             xncB[:, ic * CH:(ic + 1) * CH], ic == 0, ic == NCH - 1,
                             waits=([(s_xnl, xn_ld), (s_dve, kv_ready[s])] + bw(bank)) if ic == 0 else (),
                             inc=(s_pe.inc() if ic == NCH - 1 else None))
                    pe_n = s_pe.n
                    k.op("vector", lambda e, b=bank, oc=oc, r2=r2:
                         e.tensor_scalar(m1B[r2][:], pbk[b][:], bq_c[:, oc:oc + 1], 0.0,
                                         ALU.add, ALU.min),
                         waits=[(s_pe, pe_n), (s_act, st_b["m1B_act"][r2])], inc=s_dve.inc())
                    m1_dve = s_dve.n
                    k.op("scalar", lambda e, r2=r2: e.activation(eB[r2][:], m1B[r2][:], AF.Exp,
                                                                 bias=0.0, scale=1.0),
                         waits=[(s_dve, m1_dve), (s_dve, st_b["eB_dve"][r2])], inc=s_act.inc())
                    st_b["m1B_act"][r2] = s_act.n
                    k.op("scalar", lambda e, b=bank, oc=oc, r2=r2:
                         e.activation(rB_[r2][:], pbk[b][:], AF.Relu, bias=bq_c[:, oc:oc + 1], scale=1.0),
                         waits=[(s_dve, st_b["rB_dve"][r2])], inc=s_act.inc())
                    act_n = s_act.n
                    bank_war[bank] = (s_act, act_n)
                    k.op("vector", lambda e, oc=oc, r2=r2:
                         e.tensor_add(qP[:, oc * CH:(oc + 1) * CH], eB[r2][:], rB_[r2][:]),
                         waits=[(s_act, act_n), (s_pe, st_b["qP_pe"])], inc=s_dve.inc())
                    st_b["eB_dve"][r2] = s_dve.n
                    st_b["rB_dve"][r2] = s_dve.n
                    q_dve_oc[oc] = s_dve.n
                q_dve = s_dve.n
                for oc in range(8):
                    gb = rbank()
                    g4 = oc % 4
                    for ic in range(NCH):
                        k.mm(pbk[gb][:, :], wg[:, ic * 1024 + oc * 128: ic * 1024 + oc * 128 + 128],
                             xncB[:, ic * CH:(ic + 1) * CH], ic == 0, ic == NCH - 1,
                             waits=([(s_dve, st_b["gate_dve"][g4])] + bw(gb)) if ic == 0 else (),
                             inc=(s_pe.inc() if ic == NCH - 1 else None))
                    pe_n = s_pe.n
                    k.op("scalar", lambda e, b=gb, oc=oc, g4=g4:
                         e.activation(gateB[g4][:], pbk[b][:], AF.Sigmoid,
                                      bias=bg_c[:, oc:oc + 1], scale=1.0),
                         waits=[(s_pe, pe_n)], inc=s_act.inc())
                    gate_act = s_act.n
                    bank_war[gb] = (s_act, gate_act)
                    ab = rbank()
                    for hh in range(2):
                        h = 2 * oc + hh
                        pb_ = 64 * (h % 2)
                        fb_ = 64 * (h // 2)
                        k.mm(pbk[ab][pb_:pb_ + 64, :],
                             kv_bf[pb_:pb_ + 64, fb_:fb_ + 64],
                             qP[pb_:pb_ + 64, oc * CH:(oc + 1) * CH],
                             True, True,
                             waits=([(s_dve, q_dve_oc[oc])] + bw(ab)) if hh == 0 else (),
                             inc=(s_pe.inc() if hh == 1 else None))
                    pe_n = s_pe.n
                    k.op("vector", lambda e, b=ab, oc=oc, g4=g4:
                         e.scalar_tensor_tensor(gaB[:, oc * CH:(oc + 1) * CH], pbk[b][:],
                                                1.0, gateB[g4][:], ALU.mult, ALU.mult),
                         waits=[(s_pe, pe_n), (s_act, gate_act), (s_pe, st_b["ga_pe"])],
                         inc=s_dve.inc())
                    st_b["gate_dve"][g4] = s_dve.n
                    bank_war[ab] = (s_dve, s_dve.n)
                ga_dve = s_dve.n
                for oc in range(8):
                    j3 = oc % 3
                    k.dma("sync", xslB[j3][:], x_d[s, oc * 128:(oc + 1) * 128, c * CH:(c + 1) * CH],
                          waits=[(s_stx1[j3], s_stx1[j3].n)], inc=s_xslB[j3].inc(16))
                    x_ld = s_xslB[j3].n
                    ob = rbank()
                    for ic in range(NCH):
                        k.mm(pbk[ob][:, :], wo[:, ic * 1024 + oc * 128: ic * 1024 + oc * 128 + 128],
                             gaB[:, ic * CH:(ic + 1) * CH], ic == 0, ic == NCH - 1,
                             waits=([(s_dve, ga_dve)] + bw(ob)) if ic == 0 else (),
                             inc=(s_pe.inc() if ic == NCH - 1 else None))
                    op_pe = s_pe.n
                    fb2 = rbank()
                    for ic in range(4):
                        if oc < 4:
                            lhs1 = wrT[:, (ic * 4 + oc) * 128:(ic * 4 + oc + 1) * 128]
                            lhs2 = wiTn[:, (ic * 4 + oc) * 128:(ic * 4 + oc + 1) * 128]
                        else:
                            lhs1 = wiT[:, (ic * 4 + oc - 4) * 128:(ic * 4 + oc - 4 + 1) * 128]
                            lhs2 = wrT[:, (ic * 4 + oc - 4) * 128:(ic * 4 + oc - 4 + 1) * 128]
                        k.mm(pbk[fb2][:, :], lhs1, xncB[:, ic * CH:(ic + 1) * CH],
                             ic == 0, False,
                             waits=bw(fb2) if ic == 0 else ())
                        k.mm(pbk[fb2][:, :], lhs2, xncB[:, (4 + ic) * CH:(4 + ic + 1) * CH],
                             False, ic == 3,
                             inc=(s_pe.inc() if ic == 3 else None))
                    ffn_pe = s_pe.n
                    k.op("vector", lambda e, j=j3, oc=oc, b=ob:
                         e.scalar_tensor_tensor(xslB[j][:], pbk[b][:], bo_c[:, oc:oc + 1],
                                                xslB[j][:], ALU.add, ALU.add),
                         waits=[(s_pe, op_pe), (s_xslB[j3], x_ld)], inc=s_dve.inc())
                    bank_war[ob] = (s_dve, s_dve.n)
                    k.op("vector", lambda e, j=j3, oc=oc, b=fb2:
                         e.scalar_tensor_tensor(xslB[j][:], pbk[b][:], bffn_c[:, oc:oc + 1],
                                                xslB[j][:], ALU.add, ALU.add),
                         waits=[(s_pe, ffn_pe)], inc=s_dve.inc())
                    bank_war[fb2] = (s_dve, s_dve.n)
                    k.dma("scalar", x1_d[s, oc * 128:(oc + 1) * 128, c * CH:(c + 1) * CH],
                          xslB[j3][:], waits=[(s_dve, s_dve.n)], inc=s_stx1[j3].inc(16))
                st_b["xn_pe"] = s_pe.n
                st_b["qP_pe"] = s_pe.n
                st_b["ga_pe"] = s_pe.n
            l0b_pe_end = s_pe.n
            l0b_st_end = [s_stx1[j].n for j in range(3)]
    L0_PE_END = s_pe.n
    L0_X1_ST = [s_stx1[j].n for j in range(3)]

    L1W_LD = 0
    if "l1a" in phases or "l1b" in phases or "l1c" in phases:
        for (dst, src) in ((l1wk, l1wk_i), (l1wv, l1wv_i), (l1wq, l1wq_i),
                           (l1wg, l1wg_i), (l1wo, l1wo_i)):
            k.dma("sync", dst[:], src[:], waits=[(s_pe, L0_PE_END)], inc=s_wfc.inc(16))
        L1W_LD = s_wfc.n
    if "l1c" in phases:
        k.dma("sync", wfc[:, :], l1wfc_i[:], waits=[(s_pe, L0_PE_END)], inc=s_wfc.inc(16))
        k.dma("sync", wproj[:, 8192:32768], l1wproj_i[:, 8192:32768], inc=s_wfc.inc(16))
    WFC_HI = s_wfc.n

    l1_xn_store = {}
    xo_store_n = 0
    b1_pe_end = {}
    if "l1a" in phases:
        vq_war = {"valsb": 0, "quersb": 0}
        b1_gp_end = 0
        b1_dve_end = 0
        pj_war_dve = 0
        def heavy_l1a(s, c, xnc_dve, gl_store):
            X = XB[c % 2]
            for tt in range(CH // 128):
                trot = tt % 2
                for ic in range(NCH):
                    k.mm(pbk[4][:, 0:256],
                         X[:, ic * CH + tt * 128: ic * CH + tt * 128 + 128],
                         l1wk[:, ic * 256:(ic + 1) * 256],
                         ic == 0, ic == NCH - 1,
                         waits=([(s_dve, xnc_dve), (s_wfc, L1W_LD)] + bw(4)) if ic == 0 else (),
                         inc=(s_pe.inc() if ic == NCH - 1 else None))
                pe_n = s_pe.n
                k.op("vector", lambda e, r=trot: e.tensor_copy(keyt[r][:], pbk[4][:, 0:256]),
                     waits=[(s_pe, pe_n), (s_stkey[trot], s_stkey[trot].n)],
                     inc=s_dve.inc())
                bank_war[4] = (s_dve, s_dve.n)
                k.dma("scalar", keys_d[s, c * 4 + tt], keyt[trot][:],
                      waits=[(s_dve, s_dve.n)], inc=s_stkey[trot].inc(16))
            for (dstT, wmat, scale) in ((valsb, l1wv, None), (quersb, l1wq, 0.0625)):
                for mc in range(2):
                    bank = 5 + mc
                    for ic in range(NCH):
                        k.mm(pbk[bank][:, :],
                             wmat[:, ic * 256 + mc * 128: ic * 256 + mc * 128 + 128],
                             X[:, ic * CH:(ic + 1) * CH],
                             ic == 0, ic == NCH - 1,
                             waits=([(s_dve, xnc_dve),
                                     (s_pe, vq_war["valsb" if dstT is valsb else "quersb"])]
                                    + bw(bank)) if ic == 0 else (),
                             inc=(s_pe.inc() if ic == NCH - 1 else None))
                    pe_n = s_pe.n
                    dslice = (mc * 2048 + c * CH, mc * 2048 + (c + 1) * CH)
                    if scale is None:
                        k.op("vector", lambda e, b=bank, d=dstT, ds=dslice:
                             e.tensor_copy(d[:, ds[0]:ds[1]], pbk[b][:]),
                             waits=[(s_pe, pe_n)], inc=s_dve.inc())
                    else:
                        k.op("vector", lambda e, b=bank, d=dstT, ds=dslice, sc=scale:
                             e.tensor_scalar_mul(d[:, ds[0]:ds[1]], pbk[b][:], sc),
                             waits=[(s_pe, pe_n)], inc=s_dve.inc())
                    bank_war[bank] = (s_dve, s_dve.n)
            for ic in range(NCH):
                k.mm(pbk[7][0:1, :], l1wg[:, ic:ic + 1], X[:, ic * CH:(ic + 1) * CH],
                     ic == 0, ic == NCH - 1,
                     waits=([(s_dve, xnc_dve)] + bw(7)) if ic == 0 else (),
                     inc=(s_pe.inc() if ic == NCH - 1 else None))
            pe_n = s_pe.n
            ln_state["xbuf_war"][c % 2] = pe_n
            k.op("scalar", lambda e: e.activation(pbk[7][32:33, 0:CH], pbk[7][0:1, :],
                                                  AF.Sigmoid, bias=l1bg[0:1, 0:1], scale=1.0),
                 waits=[(s_pe, pe_n)], inc=s_act.inc())
            rot = c % 2
            k.op("scalar", lambda e, r=rot: e.activation(glog_rowc[0:1, r * 512:(r + 1) * 512],
                                                         pbk[7][32:33, 0:CH], AF.Ln,
                                                         bias=1e-8, scale=1.0),
                 waits=[(s_stgl, gl_store.get(rot, 0))], inc=s_act.inc())
            bank_war[7] = (s_act, s_act.n)
            k.dma("scalar", glog_d[s, c * CH:(c + 1) * CH],
                  glog_rowc[0:1, rot * 512:(rot + 1) * 512],
                  waits=[(s_act, s_act.n)], inc=s_stgl.inc(16))
            gl_store[rot] = s_stgl.n

        for s in range(SEQS):
            gl_store = {}
            if s == 0:
                fw0 = ([(s_stx1[j], L0_X1_ST[j]) for j in range(3)]
                       + [(s_pe, L0_PE_END)])
            else:
                fw0 = ([(s_pe, s_pe.n), (s_gp, b1_gp_end), (s_dve, b1_dve_end)]
                       + [(s_stxo[j], s_stxo[j].n) for j in range(2)])
            stats = {}
            xn_dve = {}
            for c in range(NCHUNK):
                stats[c] = ln_front(x1_d, s, c, first_waits=(fw0 if c == 0 else ()))
                xn_dve[c] = ln_back(xn1_d, s, c, stats[c])
                if c >= 1:
                    heavy_l1a(s, c - 1, xn_dve[c - 1], gl_store)
                ln_store(xn1_d, s, c, xn_dve[c])
            heavy_l1a(s, NCHUNK - 1, xn_dve[NCHUNK - 1], gl_store)
            vq_dve = s_dve.n
            l1_xn_store[s] = s_stxn.n

            k.dma("sync", pj_a[:], glog_d[s].rearrange("(p j) -> p j", p=128),
                  waits=[(s_stgl, s_stgl.n), (s_dve, pj_war_dve)], inc=s_rowl.inc(16))
            k.op("vector", lambda e: e.tensor_tensor_scan(pj_b[:], pj_a[:], zeros16[:],
                                                          0.0, ALU.add, ALU.add),
                 waits=[(s_rowl, s_rowl.n), (s_gp, GP_CONST)], inc=s_dve.inc())
            scan_dve = s_dve.n
            k.mm(pbk[7][0:128, 256:257], U_tri[:], pj_b[:, 15:16], True, True,
                 waits=[(s_dve, scan_dve)] + bw(7), inc=s_pe.inc())
            pe_n = s_pe.n
            k.op("vector", lambda e: e.tensor_copy(prefcol[:], pbk[7][0:128, 256:257]),
                 waits=[(s_pe, pe_n)], inc=s_dve.inc())
            bank_war[7] = (s_dve, s_dve.n)
            k.op("vector", lambda e: e.tensor_scalar(cumtm[:], pj_b[:], prefcol[:, 0:1], 0.0,
                                                     ALU.add, ALU.add),
                 waits=[(s_gp, b1_gp_end), (s_dve, s_dve.n)], inc=s_dve.inc())
            pj_war_dve = s_dve.n
            k.dma("scalar", cum_d[s].rearrange("(p j) -> p j", p=128), cumtm[:],
                  waits=[(s_dve, s_dve.n)], inc=s_strow.inc(16))
            cum_st = s_strow.n
            if cumdbg_d is not None and s == 0:
                for (dsl, src) in (((0, 1), prefcol), ((1, 17), pj_b),
                                   ((17, 33), pj_a), ((33, 161), U_tri),
                                   ((161, 177), cumtm)):
                    k.op("scalar", lambda e, dsl=dsl, src=src: e.dma_start(
                         out=cumdbg_d[:, dsl[0]:dsl[1]], in_=src[:],
                         allow_slow_non_contiguous=True),
                         waits=[(s_dve, s_dve.n)], inc=s_strow.inc(16))
                cum_st = s_strow.n

            if "l1b" not in phases:
                continue
            cumj_dve = 0
            for jc in range(NCHUNK):
                k.dma("sync", cumrow[0:1, :], cum_d[s, jc * CH:(jc + 1) * CH],
                      waits=[(s_strow, cum_st), (s_pe, b1_pe_end.get("cumrow", 0))],
                      inc=s_rowl.inc(16))
                k.mm(pbk[6][:, :], ones_f32[:], cumrow[0:1, :], True, True,
                     waits=[(s_rowl, s_rowl.n)] + bw(6), inc=s_pe.inc())
                pe_n = s_pe.n
                b1_pe_end["cumrow"] = pe_n
                k.op("vector", lambda e, jc=jc: e.tensor_copy(cumjb[:, jc * CH:(jc + 1) * CH],
                                                              pbk[6][:]),
                     waits=[(s_pe, pe_n), (s_dve, cumj_dve)], inc=s_dve.inc())
                bank_war[6] = (s_dve, s_dve.n)
            cumj_dve = s_dve.n
            for it in range(16):
                k.dma("sync", cumtm[:, it:it + 1], cum_d[s, it * 128:(it + 1) * 128],
                      waits=[(s_dve, s_dve.n)], inc=s_rowl.inc(16))
            cumtm_ld = s_rowl.n
            k.wait("sync", s_strow, cum_st)

            b1 = {"wt_dve": [0, 0, 0, 0], "dscr_act": [0, 0, 0, 0],
                  "dec_dve": [0, 0, 0, 0], "reads_pe": 0,
                  "exp_done": [0, 0, 0, 0],
                  "x1c_pe": s_pe.n}
            for jc in range(NCHUNK):
                nit = 4 * jc + 4

                def reads_mms(it, nit=nit, s=s):
                    klrot = it % 2
                    krot = it % 4
                    wt_dve = b1["wt_dve"][krot]
                    key_ld = b1[f"key_ld{klrot}"]
                    for mc in range(2):
                        k.mm(pbk[2 + mc][:, :], keytB[klrot][:, mc * 128:(mc + 1) * 128],
                             wtb[krot][:], it == 0, it == nit - 1,
                             waits=([(s_dve, wt_dve), (s_keyl[klrot], key_ld)]
                                    + (bw(2 + mc) if it == 0 else [])),
                             inc=(s_pe.inc() if mc == 1 else None))
                    b1[f"keyB_pe{klrot}"] = s_pe.n
                    b1[f"wt_pe{krot}"] = s_pe.n

                sc_pe = {}
                for it in range(nit):
                    krot = it % 4
                    klrot = it % 2
                    k.dma("sync", keytB[klrot][:], keys_d[s, it],
                          waits=[(s_stkey[0], s_stkey[0].n), (s_stkey[1], s_stkey[1].n),
                                 (s_pe, b1.get(f"keyB_pe{klrot}", 0)),
                                 (s_dve, vq_dve)],
                          inc=s_keyl[klrot].inc(16))
                    b1[f"key_ld{klrot}"] = s_keyl[klrot].n
                    sb_ = (0, 1, 6, 7)[it % 4]
                    b1[f"sbank{krot}"] = sb_
                    for mc in range(2):
                        k.mm(pbk[sb_][:, :], valsb[:, mc * 2048 + it * 128: mc * 2048 + it * 128 + 128],
                             quersb[:, mc * 2048 + jc * CH: mc * 2048 + (jc + 1) * CH],
                             mc == 0, mc == 1,
                             waits=([(s_dve, vq_dve)] + bw(sb_)) if mc == 0 else (),
                             inc=(s_pe.inc() if mc == 1 else None))
                    sc_pe[it] = s_pe.n
                    k.op("vector", lambda e, r=krot, jc=jc, it=it:
                         e.tensor_scalar(dscr[r][:], cumjb[:, jc * CH:(jc + 1) * CH],
                                         cumtm[:, it:it + 1], 0.0, ALU.subtract, ALU.min),
                         waits=[(s_rowl, cumtm_ld), (s_dve, cumj_dve),
                                (s_dve, b1["dec_dve"][krot])],
                         inc=s_dve.inc())
                    if it % 2 == 0:
                        continue
                    ds_n = s_dve.n
                    r0 = krot - 1
                    k.op("scalar", lambda e, r0=r0:
                         e.activation(dscr2[r0 // 2][:], dscr2[r0 // 2][:],
                                      AF.Exp, bias=0.0, scale=1.0),
                         waits=[(s_dve, ds_n)], inc=s_act.inc())
                    dec_act = s_act.n
                    for px in (it - 1, it):
                        pr = px % 4
                        dec_ready = (s_act, dec_act)
                        if px >= 4 * jc:
                            k.op("gpsimd", lambda e, r=pr, jc=jc, px=px:
                                 e.affine_select(dscr[r][:], dscr[r][:], [[1, CH]], ALU.is_ge,
                                                 0.0, base=jc * CH - px * 128, channel_multiplier=-1),
                                 waits=[dec_ready], inc=s_gp.inc())
                            dec_ready = (s_gp, s_gp.n)
                        k.op("vector", lambda e, r=pr, b=b1[f"sbank{pr}"]:
                             e.scalar_tensor_tensor(wtb[r][:], pbk[b][:], 1.0, dscr[r][:],
                                                    ALU.mult, ALU.mult),
                             waits=[dec_ready, (s_pe, sc_pe[px]),
                                    (s_pe, b1.get(f"wt_pe{pr}", 0))],
                             inc=s_dve.inc())
                        b1["dec_dve"][pr] = s_dve.n
                        bank_war[b1[f"sbank{pr}"]] = (s_dve, s_dve.n)
                        b1["wt_dve"][pr] = s_dve.n
                    reads_mms(it - 1)
                    reads_mms(it)
                reads_pe = s_pe.n
                for mc in range(2):
                    k.op("vector", lambda e, mc=mc: e.tensor_scalar_mul(
                         readsb[:, mc * 512:(mc + 1) * 512], pbk[2 + mc][:], RSQ_T),
                         waits=[(s_pe, reads_pe)], inc=s_dve.inc())
                    bank_war[2 + mc] = (s_dve, s_dve.n)
                reads_dve = s_dve.n
                ob_banks = [4, 5, 2, 3, 0, 1, 6, 7]
                op_pe = {}
                k.dma("scalar", x1cB1[:], fm_ap(x1_d, s, jc * CH, CH),
                      waits=[(s_stxo[0], b1.get("xo_st", 0)), (s_dve, vq_dve),
                             (s_pe, b1.get("x1c_pe", 0))],
                      inc=s_x1l[0].inc(16))
                x1_ld = s_x1l[0].n
                for oc in range(8):
                    ob = ob_banks[oc]
                    for mc in range(2):
                        k.mm(pbk[ob][:, :], l1wo[:, mc * 1024 + oc * 128: mc * 1024 + oc * 128 + 128],
                             readsb[:, mc * 512:(mc + 1) * 512], mc == 0, mc == 1,
                             waits=([(s_dve, reads_dve)] + bw(ob)) if mc == 0 else (),
                             inc=(s_pe.inc() if mc == 1 else None))
                    op_pe[oc] = s_pe.n
                for oc in range(8):
                    ob = ob_banks[oc]
                    k.op("vector", lambda e, oc=oc, b=ob:
                         e.tensor_add(x1cB1[:, oc * CH:(oc + 1) * CH],
                                      x1cB1[:, oc * CH:(oc + 1) * CH], pbk[b][:]),
                         waits=[(s_pe, op_pe[oc]), (s_x1l[0], x1_ld)],
                         inc=s_dve.inc())
                    bank_war[ob] = (s_dve, s_dve.n)
                    for half in range(2):
                        kc = s * 8 + jc * 2 + half
                        k.dma("scalar", xo_d[kc][:, oc * 256:(oc + 1) * 256],
                              x1cB1[:, oc * CH + half * 256: oc * CH + half * 256 + 256],
                              waits=[(s_dve, s_dve.n)], inc=s_stxo[0].inc(16))
                b1["xo_st"] = s_stxo[0].n
            vq_war["valsb"] = s_pe.n
            vq_war["quersb"] = s_pe.n
            b1_gp_end = s_gp.n
            b1_dve_end = s_dve.n
    L1AB_PE = s_pe.n
    L1AB_GP = s_gp.n
    L1AB_DVE = s_dve.n
    XO_ST = [s_stxo[j].n for j in range(2)]
    XN1_ST = s_stxn.n

    if "l1c" in phases:
        k.dma("sync", wproj[:, 0:8192], l1wproj_i[:, 0:8192],
              waits=[(s_pe, L1AB_PE)], inc=s_wfc.inc(16))
        WFC_ALL = s_wfc.n
        cst = {"xn_pe": [0, 0], "y_dve": 0, "scan_dve": 0,
               "rr_dve": [0, 0], "spk_dve": 0, "acc_dve": 0, "y_pe": 0,
               "mem_gp": 0}
        for kk in range(NKC):
            s, t0 = kk // (T // CCH), (kk % (T // CCH)) * CCH
            xrot = kk % 2
            cbar = ([(s_pe, L1AB_PE), (s_gp, L1AB_GP), (s_dve, L1AB_DVE)]
                    if kk == 0 else [])
            k.dma("sync", xnC[xrot][:], xn1_d[kk],
                  waits=[(s_stxn, XN1_ST), (s_pe, cst["xn_pe"][xrot])] + cbar,
                  inc=s_xnlC.inc(16))
            xn_ld = s_xnlC.n
            for hp in range(16):
                fb_ = (0, 1, 6, 7)[hp % 4]
                for sub in range(2):
                    hf = 2 * hp + sub
                    for ic in range(NCH):
                        k.mm(pbk[fb_][:, sub * CCH:(sub + 1) * CCH],
                             wfc[:, ic * 4096 + hf * 128: ic * 4096 + hf * 128 + 128],
                             xnC[xrot][:, ic * CCH:(ic + 1) * CCH],
                             ic == 0, ic == NCH - 1,
                             waits=([(s_xnlC, xn_ld), (s_wfc, WFC_ALL)] + bw(fb_))
                                   if (ic == 0 and sub == 0) else (),
                             inc=(s_pe.inc() if (ic == NCH - 1 and sub == 1) else None))
                pe_n = s_pe.n
                k.op("scalar", lambda e, b=fb_: e.activation(rrC[0][:], pbk[b][:], AF.Relu,
                                                             bias=0.0, scale=1.0),
                     waits=[(s_pe, pe_n)], inc=s_act.inc())
                bank_war[fb_] = (s_act, s_act.n)
                k.op("scalar", lambda e, hp=hp:
                     e.activation(hC[:, hp * 512:(hp + 1) * 512], rrC[0][:], AF.Square,
                                  bias=0.0, scale=1.0),
                     inc=s_act.inc())
            h_gp = s_act.n
            cst["xn_pe"][xrot] = s_pe.n
            for oc in range(8):
                yb = 2 + oc // 2
                reg = (oc % 2) * CCH
                for hf in range(32):
                    k.mm(pbk[yb][:, reg:reg + CCH],
                         wproj[:, hf * 1024 + oc * 128: hf * 1024 + oc * 128 + 128],
                         hC[:, hf * CCH:(hf + 1) * CCH],
                         hf == 0, hf == 31,
                         waits=([(s_act, h_gp)] + (bw(yb) if oc % 2 == 0 else [])) if hf == 0 else (),
                         inc=(s_pe.inc() if hf == 31 else None))
            y_pe = s_pe.n
            cst["y_pe"] = y_pe
            for j in range(4):
                k.op("vector", lambda e, j=j: e.tensor_copy(yC[:, j * 512:(j + 1) * 512],
                                                            pbk[2 + j][:]),
                     waits=[(s_pe, y_pe)] + ([(s_dve, cst["scan_dve"])] if j == 0 else []),
                     inc=s_dve.inc())
                bank_war[2 + j] = (s_dve, s_dve.n)
            y_dve = s_dve.n
            k.dma("gpsimd", xoC[:], xo_d[kk],
                  waits=[(s_stxo[0], XO_ST[0]), (s_stxo[1], XO_ST[1]),
                         (s_sty, s_sty.n)], inc=s_xol.inc(16))
            xo_ld = s_xol.n
            k.op("vector", lambda e: e.tensor_single_scalar(spkC[:], yC[:], 1.0, ALU.is_gt))
            k.op("vector", lambda e: e.tensor_copy(accC[:], spkC[:]))
            for step in range(2, 6):
                if step == 2:
                    k.op("vector", lambda e: e.scalar_tensor_tensor(memC[:], yC[:], 0.9, yC[:],
                                                                    ALU.mult, ALU.add))
                else:
                    k.op("vector", lambda e: e.scalar_tensor_tensor(memC[:], memC[:], 0.9, yC[:],
                                                                    ALU.mult, ALU.add))
                k.op("vector", lambda e: e.tensor_sub(memC[:], memC[:], spkC[:]))
                k.op("vector", lambda e: e.tensor_single_scalar(spkC[:], memC[:], 1.0, ALU.is_gt))
                k.op("vector", lambda e: e.tensor_add(accC[:], accC[:], spkC[:]))
            k.op("vector", lambda e: e.scalar_tensor_tensor(xoC[:], accC[:], 0.2, xoC[:],
                                                            ALU.mult, ALU.add),
                 waits=[(s_xol, xo_ld)], inc=s_dve.inc())
            cst["scan_dve"] = s_dve.n
            cst["acc_dve"] = s_dve.n
            cst["spk_dve"] = s_dve.n
            k.dma("gpsimd", fm_ap(y_d, s, t0, CCH), xoC[:],
                  waits=[(s_dve, s_dve.n)], inc=s_sty.inc(16))

        k.wait("gpsimd", s_sty, s_sty.n)

    k.emit(ctx)
    nc._ctx = ctx
    return nc


_cache = {}


def kernel(**inputs):
    w = prep_weights(inputs)
    x = np.asarray(inputs["L0_attn_x"], np.float32)
    if "prog" not in _cache:
        _cache["prog"] = build_program()
    nc = _cache["prog"]
    in_maps = []
    for c in range(NCORES):
        xs = np.ascontiguousarray(x[c * SEQS:(c + 1) * SEQS].transpose(0, 2, 1))
        m = {"x": xs}
        m.update({kk: vv for kk, vv in w.items() if vv is not None})
        in_maps.append(m)
    res = run_bass_kernel_spmd(nc, in_maps, core_ids=list(range(NCORES)))
    outs = [r["y"].transpose(0, 2, 1) for r in res.results]
    return np.ascontiguousarray(np.concatenate(outs, axis=0))


# version 24
# speedup vs baseline: 1.0414x; 1.0414x over previous
import contextlib
import numpy as np
import ml_dtypes
import concourse.bass as bass
import concourse.mybir as mybir
from concourse.bass_utils import run_bass_kernel_spmd

F32 = mybir.dt.float32
BF16 = mybir.dt.bfloat16
AF = mybir.ActivationFunctionType
ALU = mybir.AluOpType
bf16 = ml_dtypes.bfloat16

NCORES = 8
SEQS = 4
T = 2048
D = 1024
NCH = 8
CH = 512
NCHUNK = T // CH
HEADS = 16
M1 = 256
HID = 4096
CCH = 256
NKC = SEQS * T // CCH
EPS = 1e-5
RSQ_T = 1.0 / float(np.sqrt(np.float32(T)))



def prep_weights(i):
    w = {}

    def wchunks(a, nout):
        a = np.asarray(a, np.float32)
        return np.ascontiguousarray(
            a.reshape(8, 128, nout).transpose(1, 0, 2).reshape(128, 8 * nout)).astype(bf16)

    for nm, key in (("wq", "L0_attn_Wq"), ("wk", "L0_attn_Wk"),
                    ("wv", "L0_attn_Wv"), ("wg", "L0_attn_Wg"), ("wo", "L0_attn_Wo")):
        w[nm] = wchunks(i[key], 1024)

    def bcol(b, n=8):
        return np.ascontiguousarray(np.asarray(b, np.float32).reshape(n, 128).T)

    w["bq_c"] = bcol(i["L0_attn_bq"])
    w["bg_c"] = bcol(i["L0_attn_bg"])
    w["bo_c"] = bcol(i["L0_attn_bo"])
    w["bffn_c"] = bcol(np.concatenate([np.asarray(i["L0_ffn_bias_real"], np.float32),
                                       np.asarray(i["L0_ffn_bias_imag"], np.float32)]))
    w["bk_r"] = np.asarray(i["L0_attn_bk"], np.float32).reshape(1, 1024).astype(bf16)
    w["bv_r"] = np.asarray(i["L0_attn_bv"], np.float32).reshape(1, 1024).astype(bf16)

    def g1m(g1):
        a = np.asarray(g1, np.float32)[0]
        return np.ascontiguousarray(a.transpose(1, 0, 2).reshape(4, 1024))

    def g2m(g2):
        a = np.asarray(g2, np.float32)[:, :, 0, :]
        return np.ascontiguousarray(a.transpose(0, 2, 1).reshape(4, 256))

    w["g1r"] = g1m(i["L0_ffn_G1_R"]); w["g2r"] = g2m(i["L0_ffn_G2_R"])
    w["g1i"] = g1m(i["L0_ffn_G1_I"]); w["g2i"] = g2m(i["L0_ffn_G2_I"])

    w["l1_wk"] = wchunks(np.asarray(i["L1_attn_Wk"], np.float32).T, 256)
    w["l1_wv"] = wchunks(np.asarray(i["L1_attn_Wv"], np.float32).T, 256)
    w["l1_wq"] = wchunks(np.asarray(i["L1_attn_Wq"], np.float32).T, 256)
    w["l1_wg"] = wchunks(np.asarray(i["L1_attn_Wg"], np.float32).T, 1)
    w["l1_bg"] = np.asarray(i["L1_attn_bg"], np.float32).reshape(1, 1)
    w["l1_wo"] = wchunks(np.asarray(i["L1_attn_Wo"], np.float32).T.reshape(2, 128, 1024)
                         .reshape(256, 1024), 1024)[:, :2048] if False else None
    a = np.asarray(i["L1_attn_Wo"], np.float32).T
    w["l1_wo"] = np.ascontiguousarray(
        a.reshape(2, 128, 1024).transpose(1, 0, 2).reshape(128, 2048)).astype(bf16)
    w["l1_wfc"] = wchunks(np.asarray(i["L1_ffn_W_fc"], np.float32).T, 4096)
    a = np.asarray(i["L1_ffn_W_proj"], np.float32).T
    w["l1_wproj"] = np.ascontiguousarray(
        a.reshape(32, 128, 1024).transpose(1, 0, 2).reshape(128, 32768)).astype(bf16)
    return w



class Sem:
    __slots__ = ("h", "n", "name")

    def __init__(self, h, name):
        self.h = h
        self.n = 0
        self.name = name

    def inc(self, k=1):
        self.n += k
        return (self, k)


class K:
    ENGINES = ("sync", "tensor", "vector", "scalar", "gpsimd")

    def __init__(self, nc):
        self.nc = nc
        self.ops = {e: [] for e in self.ENGINES}
        self._lastwait = {e: {} for e in self.ENGINES}

    def sem(self, ctx, name):
        return Sem(ctx.enter_context(self.nc.semaphore(name)), name)

    def wait(self, eng, sem, val):
        if sem is None or val <= 0:
            return
        lw = self._lastwait[eng]
        if lw.get(sem, -1) >= val:
            return
        lw[sem] = val
        self.ops[eng].append(("w", sem, val))

    def op(self, eng, build, waits=(), inc=None):
        for (s, v) in waits:
            self.wait(eng, s, v)
        incs = []
        if inc is not None:
            incs = [inc] if isinstance(inc, tuple) else [x for x in inc if x]
        self.ops[eng].append(("o", build, incs))

    def dma(self, queue, out, in_, waits=(), inc=None):
        self.op(queue, lambda e, o=out, i=in_: e.dma_start(out=o, in_=i), waits, inc)

    def mm(self, out, lhsT, rhs, start, stop, waits=(), inc=None):
        self.op("tensor",
                lambda e, o=out, l=lhsT, r=rhs, st=start, sp=stop:
                e.matmul(o, l, r, start=st, stop=sp),
                waits, inc)

    def emit(self, ctx):
        block = ctx.enter_context(self.nc.Block())
        for eng_name in self.ENGINES:
            items = self.ops[eng_name]
            if not items:
                continue

            def body(eng, items=items):
                for it in items:
                    if it[0] == "w":
                        eng.wait_ge(it[1].h, it[2])
                    else:
                        ins = it[1](eng)
                        for (s, kk) in it[2]:
                            ins = ins.then_inc(s.h, kk)
            getattr(block, eng_name)(body)



def build_program(debug=(), phases=("wbuild", "l0a", "l0b", "l1a", "l1b", "l1c")):
    nc = bass.Bass("TRN2", target_bir_lowering=False)
    ctx = contextlib.ExitStack()

    def kind(name):
        return "ExternalOutput" if name in debug else "Internal"

    x_d = nc.dram_tensor("x", [SEQS, D, T], F32, kind="ExternalInput")
    y_d = nc.dram_tensor("y", [SEQS, D, T], F32, kind="ExternalOutput")
    x1_d = nc.dram_tensor("x1", [SEQS, D, T], F32, kind=kind("x1"))
    xo_d = nc.dram_tensor("xo", [NKC, 128, 2048], F32, kind=kind("xo"))
    xn0_d = nc.dram_tensor("xn0", [SEQS, NCHUNK, 128, 4096], BF16, kind=kind("xn0"))
    xn1_d = nc.dram_tensor("xn1", [NKC, 128, 2048], BF16, kind=kind("xn1"))
    keys_d = nc.dram_tensor("keysd", [SEQS, 16, 128, M1], BF16, kind=kind("keysd"))
    glog_d = nc.dram_tensor("glogd", [SEQS, T], F32, kind=kind("glogd"))
    cum_d = nc.dram_tensor("cumd", [SEQS, T], F32, kind=kind("cumd"))
    t1r_d = nc.dram_tensor("t1r", [D, 256], BF16, kind=kind("t1r"))
    t1i_d = nc.dram_tensor("t1i", [D, 256], BF16, kind=kind("t1i"))
    kvdbg_d = (nc.dram_tensor("kvdbg", [SEQS, 128, 512], F32, kind="ExternalOutput")
               if "kvdbg" in debug else None)
    cumdbg_d = (nc.dram_tensor("cumdbg", [128, 177], F32, kind="ExternalOutput")
                if "cumdbg" in debug else None)

    def din(name, shape, dt):
        return nc.dram_tensor(name, shape, dt, kind="ExternalInput")

    wq_i = din("wq", [128, 8192], BF16); wk_i = din("wk", [128, 8192], BF16)
    wv_i = din("wv", [128, 8192], BF16); wg_i = din("wg", [128, 8192], BF16)
    wo_i = din("wo", [128, 8192], BF16)
    bq_i = din("bq_c", [128, 8], F32); bg_i = din("bg_c", [128, 8], F32)
    bo_i = din("bo_c", [128, 8], F32); bffn_i = din("bffn_c", [128, 8], F32)
    bk_i = din("bk_r", [1, 1024], BF16); bv_i = din("bv_r", [1, 1024], BF16)
    g1r_i = din("g1r", [4, 1024], F32); g2r_i = din("g2r", [4, 256], F32)
    g1i_i = din("g1i", [4, 1024], F32); g2i_i = din("g2i", [4, 256], F32)
    l1wk_i = din("l1_wk", [128, 2048], BF16); l1wv_i = din("l1_wv", [128, 2048], BF16)
    l1wq_i = din("l1_wq", [128, 2048], BF16); l1wg_i = din("l1_wg", [128, 8], BF16)
    l1bg_i = din("l1_bg", [1, 1], F32)
    l1wo_i = din("l1_wo", [128, 2048], BF16)
    l1wfc_i = din("l1_wfc", [128, 32768], BF16)
    l1wproj_i = din("l1_wproj", [128, 32768], BF16)

    AW_BYTES = 131072
    AA_BYTES = 61440
    aw_t = ctx.enter_context(nc.sbuf_tensor("arena_w", [128, AW_BYTES], mybir.dt.uint8))
    aa_t = ctx.enter_context(nc.sbuf_tensor("arena_a", [128, AA_BYTES], mybir.dt.uint8))
    aw_base = nc.lookup_mloc(aw_t).addr
    aa_base = nc.lookup_mloc(aa_t).addr

    def at_w(name, shape, dt, off):
        per = int(np.prod(shape[1:])) * mybir.dt.size(dt)
        assert off + per <= AW_BYTES, (name, off, per)
        return nc.alloc_sbuf_tensor_at(name, shape, dt, offset=aw_base + off)

    def at_a(name, shape, dt, off):
        per = int(np.prod(shape[1:])) * mybir.dt.size(dt)
        assert off + per <= AA_BYTES, (name, off, per)
        return nc.alloc_sbuf_tensor_at(name, shape, dt, offset=aa_base + off)

    wq = at_w("wq_s", [128, 8192], BF16, 0)
    wk = at_w("wk_s", [128, 8192], BF16, 16384)
    wv = at_w("wv_s", [128, 8192], BF16, 32768)
    wg = at_w("wg_s", [128, 8192], BF16, 49152)
    wo = at_w("wo_s", [128, 8192], BF16, 65536)
    wrT = at_w("wrT", [128, 2048], BF16, 81920)
    wiT = at_w("wiT", [128, 2048], BF16, 86016)
    wiTn = at_w("wiTn", [128, 2048], BF16, 90112)
    g1r = at_w("g1r_s", [4, 1024], F32, 94208)
    g2r = at_w("g2r_s", [4, 256], F32, 98304)
    g1i = at_w("g1i_s", [4, 1024], F32, 99328)
    g2i = at_w("g2i_s", [4, 256], F32, 103424)
    bkb = at_w("bkb", [128, 1024], BF16, 104448)
    bvb = at_w("bvb", [128, 1024], BF16, 106496)
    tmpw = at_w("tmpw", [128, 256], BF16, 108544)
    bk_r = at_w("bk_rs", [1, 1024], BF16, 109056)
    bv_r = at_w("bv_rs", [1, 1024], BF16, 111104)
    wfc = at_w("wfc_s", [128, 32768], BF16, 0)
    wproj = at_w("wproj_s", [128, 32768], BF16, 65536)
    l1wk = at_w("l1wk_s", [128, 2048], BF16, 65536)
    l1wv = at_w("l1wv_s", [128, 2048], BF16, 69632)
    l1wq = at_w("l1wq_s", [128, 2048], BF16, 73728)
    l1wo = at_w("l1wo_s", [128, 2048], BF16, 77824)
    def sbst(name, shape, dt):
        return ctx.enter_context(nc.sbuf_tensor(name, shape, dt))

    l1wg = sbst("l1wg_s", [128, 8], BF16)

    l1bg = sbst("l1bg_s", [1, 1], F32)
    bq_c = sbst("bq_cs", [128, 8], F32)
    bg_c = sbst("bg_cs", [128, 8], F32)
    bo_c = sbst("bo_cs", [128, 8], F32)
    bffn_c = sbst("bffn_cs", [128, 8], F32)
    ones_bf = sbst("ones_bf", [1, 128], BF16)
    ones_f32 = sbst("ones_f32", [1, 128], F32)
    ones_col = sbst("ones_col", [128, 1], BF16)
    U_tri = sbst("U_tri", [128, 128], F32)
    kv_acc = sbst("kv_acc", [128, 512], F32)
    kv_bf = sbst("kv_bf", [128, 512], BF16)
    m_rowc = sbst("m_rowc", [1, 1024], BF16)
    rstd_rowc = sbst("rstd_rowc", [1, 1024], F32)
    glog_rowc = sbst("glog_rowc", [1, 1024], F32)
    cumrow = sbst("cumrow", [1, 512], F32)
    ln_row = sbst("ln_row", [1, 512], F32)
    prefcol = sbst("prefcol", [128, 1], F32)
    pj_a = sbst("pj_a", [128, 16], F32)
    pj_b = sbst("pj_b", [128, 16], F32)
    cumtm = sbst("cumtm", [128, 16], F32)
    zeros16 = sbst("zeros16", [128, 16], F32)
    c_eps = sbst("c_eps", [128, 1], F32)
    c_eps8 = sbst("c_eps8", [128, 1], F32)
    nc.const_aps.register(F32, EPS, c_eps[:, :])
    nc.const_aps.register(F32, 1e-8, c_eps8[:, :])

    xhalf = at_a("xhalf", [128, 2048], F32, 0)
    xb = at_a("xb", [128, 4096], BF16, 8192)
    xnc = at_a("xnc", [128, 4096], BF16, 16384)
    mb_s = at_a("mb_s", [128, 512], BF16, 24576)
    xsq = at_a("xsq", [128, 4096], BF16, 47104)
    xsq_st = at_a("xsq_st", [128, 2048], F32, 47104)
    rb_s = at_a("rb_s", [128, 512], F32, 55296)
    u_s = [at_a(f"u_s{j}", [128, 512], F32, 25600 + 2048 * j) for j in range(2)]
    m1_s = at_a("m1_s", [128, 512], F32, 29696)
    e_s = at_a("e_s", [128, 512], F32, 31744)
    r_s = at_a("r_s", [128, 512], F32, 33792)
    kP = [at_a(f"kP{j}", [128, 1024], BF16, 35840 + 2048 * j) for j in range(2)]
    vP = [at_a(f"vP{j}", [128, 1024], BF16, 39936 + 2048 * j) for j in range(2)]
    valsb = at_a("valsb", [128, 4096], BF16, 25600)
    quersb = at_a("quersb", [128, 4096], BF16, 33792)
    keyt = [at_a(f"keyt{j}", [128, 256], BF16, 44032 + 512 * j) for j in range(2)]
    xncB = at_a("xncB", [128, 4096], BF16, 0)
    xslB = [at_a(f"xslB{j}", [128, 512], F32, 8192 + 2048 * j) for j in range(3)]
    qP = at_a("qP", [128, 4096], BF16, 14336)
    gateB = [at_a(f"gateB{j}", [128, 512], F32, 22528 + 2048 * j) for j in range(4)]
    gaB = at_a("gaB", [128, 4096], BF16, 30720)
    m1B = [at_a(f"m1B{j}", [128, 512], F32, 38912 + 2048 * j) for j in range(2)]
    eB = [at_a(f"eB{j}", [128, 512], F32, 43008 + 2048 * j) for j in range(2)]
    rB_ = [at_a(f"rB_{j}", [128, 512], F32, 47104 + 2048 * j) for j in range(2)]
    cumjb = at_a("cumjb", [128, 2048], F32, 0)
    x1cB1 = at_a("x1cB1", [128, 4096], F32, 8192)
    keytB = [at_a(f"keytB{j}", [128, 256], BF16, 24576 + 512 * j) for j in range(2)]
    dscr = [at_a(f"dscr{j}", [128, 512], F32, 41984 + 2048 * j) for j in range(4)]
    dscr2 = [at_a(f"dscr2_{j}", [128, 1024], F32, 41984 + 4096 * j) for j in range(2)]
    wtb = [at_a(f"wtb{j}", [128, 512], BF16, 50176 + 1024 * j) for j in range(4)]
    readsb = at_a("readsb", [128, 1024], BF16, 54272)
    xnC = [at_a(f"xnC{j}", [128, 2048], BF16, 4096 * j) for j in range(2)]
    hC = at_a("hC", [128, 8192], BF16, 8192)
    yC = at_a("yC", [128, 2048], F32, 24576)
    memC = at_a("memC", [128, 2048], F32, 32768)
    spkC = at_a("spkC", [128, 2048], BF16, 40960)
    accC = at_a("accC", [128, 2048], BF16, 45056)
    xoC = at_a("xoC", [128, 2048], F32, 49152)
    rrC = [at_a(f"rrC{j}", [128, 512], F32, 57344 + 2048 * j) for j in range(2)]

    pbk = [ctx.enter_context(nc.psum_tensor(f"pb{j}", [128, 512], F32)) for j in range(8)]

    k = K(nc)
    s_pe = k.sem(ctx, "s_pe")
    s_dve = k.sem(ctx, "s_dve")
    s_act = k.sem(ctx, "s_act")
    s_gp = k.sem(ctx, "s_gp")
    s_wl = k.sem(ctx, "s_wl")
    s_xsl = [k.sem(ctx, f"s_xsl{j}") for j in range(2)]
    s_xcp = k.sem(ctx, "s_xcp")
    s_xnl = k.sem(ctx, "s_xnl")
    s_xslB = [k.sem(ctx, f"s_xslB{j}") for j in range(3)]
    s_stxn = k.sem(ctx, "s_stxn")
    s_stx1 = [k.sem(ctx, f"s_stx1{j}") for j in range(3)]
    s_stkey = [k.sem(ctx, f"s_stkey{j}") for j in range(2)]
    s_stgl = k.sem(ctx, "s_stgl")
    s_strow = k.sem(ctx, "s_strow")
    s_stt1 = k.sem(ctx, "s_stt1")
    s_keyl = [k.sem(ctx, f"s_keyl{j}") for j in range(2)]
    s_x1l = [k.sem(ctx, f"s_x1l{j}") for j in range(4)]
    s_rowl = k.sem(ctx, "s_rowl")
    s_wfc = k.sem(ctx, "s_wfc")
    s_xnlC = k.sem(ctx, "s_xnlC")
    s_xol = k.sem(ctx, "s_xol")
    s_sty = k.sem(ctx, "s_sty")
    s_stxo = [k.sem(ctx, f"s_stxo{j}") for j in range(2)]

    def fm_ap(dram, s, t0, tlen):
        return dram[s, :, t0:t0 + tlen].rearrange("(ic p) t -> p ic t", p=128)

    for (dst, src) in ((wq, wq_i), (wk, wk_i), (wv, wv_i), (wg, wg_i), (wo, wo_i),
                       (g1r, g1r_i), (g2r, g2r_i), (g1i, g1i_i), (g2i, g2i_i),
                       (bk_r, bk_i), (bv_r, bv_i), (l1bg, l1bg_i),
                       (bq_c, bq_i), (bg_c, bg_i), (bo_c, bo_i), (bffn_c, bffn_i)):
        k.dma("sync", dst[:], src[:], inc=s_wl.inc(16))
    WL0 = s_wl.n

    for t, v in ((ones_bf, 1.0), (ones_f32, 1.0), (ones_col, 1.0), (U_tri, 1.0),
                 (zeros16, 0.0), (c_eps, EPS), (c_eps8, 1e-8)):
        k.op("gpsimd", lambda e, t=t, v=v: e.memset(t[:], v))
    k.op("gpsimd", lambda e: e.affine_select(U_tri[:], U_tri[:], [[1, 128]], ALU.is_gt,
                                             0.0, base=0, channel_multiplier=-1),
         inc=s_gp.inc())
    GP_CONST = s_gp.n

    WBUILD_DVE = 0
    if "wbuild" in phases and "skip_bcast" not in debug:
        for half in range(2):
            k.mm(pbk[6][:, :], ones_bf[:], bk_r[0:1, half * 512:(half + 1) * 512],
                 True, True, waits=[(s_wl, WL0), (s_gp, GP_CONST), (s_dve, s_dve.n)])
            k.mm(pbk[7][:, :], ones_bf[:], bv_r[0:1, half * 512:(half + 1) * 512],
                 True, True, inc=s_pe.inc())
            pe_n = s_pe.n
            k.op("vector", lambda e, h=half: e.tensor_copy(bkb[:, h * 512:(h + 1) * 512], pbk[6][:]),
                 waits=[(s_pe, pe_n)])
            k.op("vector", lambda e, h=half: e.tensor_copy(bvb[:, h * 512:(h + 1) * 512], pbk[7][:]),
                 inc=s_dve.inc())
    if "wbuild" in phases and "skip_t1" not in debug:
        for (g1_s, g2_s, t1_t) in ((g1r, g2r, t1r_d), (g1i, g2i, t1i_d)):
            for oc in range(8):
                k.mm(pbk[5][:, 0:256], g1_s[:, oc * 128:(oc + 1) * 128], g2_s[:],
                     True, True,
                     waits=[(s_wl, WL0), (s_dve, s_dve.n), (s_stt1, s_stt1.n)],
                     inc=s_pe.inc())
                pe_n = s_pe.n
                k.op("vector", lambda e: e.tensor_copy(tmpw[:], pbk[5][:, 0:256]),
                     waits=[(s_pe, pe_n)], inc=s_dve.inc())
                k.dma("scalar", t1_t[oc * 128:(oc + 1) * 128, :], tmpw[:],
                      waits=[(s_dve, s_dve.n)], inc=s_stt1.inc(16))
        T1_DONE = s_stt1.n
        for (t1_t, w_t) in ((t1r_d, wrT), (t1i_d, wiT)):
            rr_ap = t1_t.ap().rearrange("(b d) (g e) -> d g b e", b=32, d=32, g=16, e=16)
            for ic in range(4):
                for oc in range(4):
                    k.dma("sync", w_t[:, (ic * 4 + oc) * 128:(ic * 4 + oc + 1) * 128],
                          rr_ap[ic * 8:(ic + 1) * 8, :, oc * 8:(oc + 1) * 8, :],
                          waits=[(s_stt1, T1_DONE)], inc=s_wl.inc(16))
        WT_DONE = s_wl.n
        k.op("vector", lambda e: e.tensor_scalar_mul(wiTn[:], wiT[:], -1.0),
             waits=[(s_wl, WT_DONE)], inc=s_dve.inc())
        WBUILD_DVE = s_dve.n

    XB = [xb, xnc]
    ln_state = {"cast_war": 0, "stats_pe": 0, "pb0_act": 0, "pb0_dve": 0,
                "pb1_act": 0, "pb2_dve": 0, "pb3_dve": 0,
                "mrow_pe": 0, "rrow_pe": 0, "xbuf_war": [0, 0], "sq_pe": 0,
                "kv_in": [0, 0], "kP_war": [0, 0]}

    def ln_front(src_d, s, c, first_waits=()):
        X = XB[c % 2]
        war0 = ln_state["cast_war"]
        for h in range(2):
            stg = xhalf if h == 0 else xsq_st
            lw = [(s_xcp, war0)] + list(first_waits)
            if h == 1:
                lw.append((s_pe, ln_state["sq_pe"]))
            k.dma("sync", stg[:],
                  src_d[s, h * 512:(h + 1) * 512, c * CH:(c + 1) * CH]
                  .rearrange("(ic p) t -> p ic t", p=128),
                  waits=lw, inc=s_xsl[0].inc(16))
            k.op("vector",
                 lambda e, h=h, X=X, stg=stg:
                 e.tensor_copy(X[:, h * 2048:(h + 1) * 2048], stg[:]),
                 waits=[(s_xsl[0], s_xsl[0].n), (s_pe, ln_state["xbuf_war"][c % 2])],
                 inc=s_xcp.inc())
        ln_state["cast_war"] = s_xcp.n
        cast_done = s_xcp.n
        for ic in range(NCH):
            k.mm(pbk[0][0:1, :], ones_col[:], X[:, ic * CH:(ic + 1) * CH],
                 ic == 0, ic == NCH - 1,
                 waits=([(s_xcp, cast_done), (s_act, ln_state["pb0_act"]),
                         (s_dve, ln_state["pb0_dve"])] if ic == 0 else ()),
                 inc=(s_pe.inc() if ic == NCH - 1 else None))
        ln_state["xstats_pe"] = s_pe.n
        k.op("scalar",
             lambda e, X=X: e.activation(xsq[:], X[:], AF.Square, bias=0.0, scale=1.0),
             waits=[(s_xcp, cast_done), (s_pe, ln_state["sq_pe"])], inc=s_act.inc())
        sq_act = s_act.n
        for ic in range(NCH):
            k.mm(pbk[0][32:33, :], ones_col[:], xsq[:, ic * CH:(ic + 1) * CH],
                 ic == 0, ic == NCH - 1,
                 waits=([(s_act, sq_act)] if ic == 0 else ()),
                 inc=(s_pe.inc() if ic == NCH - 1 else None))
        ln_state["stats_pe"] = s_pe.n
        ln_state["sq_pe"] = s_pe.n
        return s_pe.n

    def ln_back(xn_dst_d, s, c, stats_done):
        X = XB[c % 2]
        rot = c % 2
        k.op("vector", lambda e, r=rot: e.tensor_scalar_mul(m_rowc[0:1, r * 512:(r + 1) * 512],
                                                            pbk[0][0:1, :], 1.0 / 1024),
             waits=[(s_pe, ln_state["xstats_pe"]), (s_pe, ln_state["mrow_pe"])],
             inc=s_dve.inc())
        mrow_dve = s_dve.n
        k.op("scalar", lambda e: e.activation(ln_row[0:1, :], pbk[0][0:1, :], AF.Square,
                                              bias=0.0, scale=1.0 / 1024),
             waits=[(s_pe, stats_done)], inc=s_act.inc())
        k.op("vector", lambda e: e.scalar_tensor_tensor(pbk[1][0:1, 0:CH], pbk[0][32:33, :],
                                                        1.0 / 1024, ln_row[0:1, :],
                                                        ALU.mult, ALU.subtract),
             waits=[(s_act, s_act.n), (s_act, ln_state["pb1_act"]), (s_pe, stats_done)],
             inc=s_dve.inc())
        stt_dve = s_dve.n
        k.op("scalar", lambda e: e.activation(pbk[1][64:65, 0:CH], pbk[1][0:1, 0:CH], AF.Ln,
                                              bias=EPS, scale=1.0),
             waits=[(s_dve, s_dve.n)], inc=s_act.inc())
        k.op("scalar", lambda e, r=rot: e.activation(rstd_rowc[0:1, r * 512:(r + 1) * 512],
                                                     pbk[1][64:65, 0:CH], AF.Exp,
                                                     bias=0.0, scale=-0.5),
             waits=[(s_pe, ln_state["rrow_pe"])], inc=s_act.inc())
        rows_act = s_act.n
        ln_state["pb1_act"] = rows_act
        rows_dve = mrow_dve
        ln_state["pb0_act"] = rows_act
        ln_state["pb0_dve"] = stt_dve
        k.mm(pbk[2][:, :], ones_bf[:], m_rowc[0:1, rot * 512:(rot + 1) * 512], True, True,
             waits=[(s_dve, mrow_dve), (s_dve, ln_state["pb2_dve"])])
        ln_state["mrow_pe"] = s_pe.n + 1
        k.mm(pbk[3][:, :], ones_f32[:], rstd_rowc[0:1, rot * 512:(rot + 1) * 512], True, True,
             waits=[(s_act, rows_act), (s_dve, ln_state["pb3_dve"])], inc=s_pe.inc())
        ln_state["rrow_pe"] = s_pe.n
        bc_pe = s_pe.n
        k.op("vector", lambda e: e.tensor_copy(mb_s[:], pbk[2][:]),
             waits=[(s_pe, bc_pe)], inc=s_dve.inc())
        ln_state["pb2_dve"] = s_dve.n
        k.op("vector", lambda e: e.tensor_copy(rb_s[:], pbk[3][:]), inc=s_dve.inc())
        ln_state["pb3_dve"] = s_dve.n
        mbrb_dve = s_dve.n
        X_v = X[:].rearrange("p (r t) -> p r t", r=NCH)
        mb_b = bass.AP(mb_s, 0, [[512, 128], [0, NCH], [1, CH]])
        rb_b = bass.AP(rb_s, 0, [[512, 128], [0, NCH], [1, CH]])
        k.op("vector", lambda e, X_v=X_v: e.tensor_tensor(X_v, X_v, mb_b, ALU.subtract),
             waits=[(s_dve, mbrb_dve)], inc=None)
        k.op("vector", lambda e, X_v=X_v: e.tensor_tensor(X_v, X_v, rb_b, ALU.mult),
             inc=s_dve.inc())
        xnc_dve = s_dve.n
        return xnc_dve

    def ln_store(xn_dst_d, s, c, xnc_dve):
        X = XB[c % 2]
        if xn_dst_d is xn0_d:
            k.dma("scalar", xn0_d[s, c], X[:], waits=[(s_dve, xnc_dve)],
                  inc=s_stxn.inc(16))
        elif xn_dst_d is xn1_d:
            xv = X[:].rearrange("p (ic t) -> p ic t", ic=NCH)
            for half in range(2):
                kc = s * 8 + c * 2 + half
                k.dma("scalar", xn1_d[kc].rearrange("p (ic t) -> p ic t", ic=NCH),
                      xv[:, :, half * 256:(half + 1) * 256],
                      waits=[(s_dve, xnc_dve)], inc=s_stxn.inc(16))

    kv_ready = {}
    l0_xn_store = {}
    bank_war = {j: (None, 0) for j in range(8)}

    def bw(j):
        sem, n = bank_war[j]
        return [(sem, n)] if sem else []

    l0b_pe_end = 0
    l0b_st_end = [0, 0, 0]
    if "l0a" in phases or "l0b" in phases:

        def heavy_l0a(s, c, xnc_dve):
            X = XB[c % 2]

            def kv_mms(tt):
                trot = tt % 2
                kv_in = ln_state["kv_in"][trot]
                tile_idx = c * (CH // 128) + tt
                bank = 6 + tile_idx % 2
                for h in range(HEADS):
                    pb_, fb_ = 64 * (h % 2), 64 * (h // 2)
                    k.mm(pbk[bank][pb_:pb_ + 64, fb_:fb_ + 64],
                         kP[trot][:, 64 * h: 64 * h + 64],
                         vP[trot][:, 64 * h: 64 * h + 64],
                         True, True,
                         waits=([(s_dve, kv_in)] + bw(bank)) if h == 0 else (),
                         inc=(s_pe.inc() if h == HEADS - 1 else None))
                pe_n = s_pe.n
                if tile_idx == 0:
                    k.op("vector", lambda e, b=bank: e.tensor_copy(kv_acc[:], pbk[b][:]),
                         waits=[(s_pe, pe_n), (s_pe, l0b_pe_end)], inc=s_dve.inc())
                else:
                    k.op("vector", lambda e, b=bank: e.tensor_add(kv_acc[:], kv_acc[:], pbk[b][:]),
                         waits=[(s_pe, pe_n)], inc=s_dve.inc())
                bank_war[bank] = (s_dve, s_dve.n)
                ln_state["kP_war"][trot] = s_pe.n

            for tt in range(CH // 128):
                trot = tt % 2
                for (wmat, dstP, bias_b, is_k) in ((wk, kP[trot], bkb, True),
                                                   (wv, vP[trot], bvb, False)):
                    for half in range(2):
                        bank = 4 + half
                        for ic in range(NCH):
                            k.mm(pbk[bank][:, :],
                                 X[:, ic * CH + tt * 128: ic * CH + tt * 128 + 128],
                                 wmat[:, ic * 1024 + half * 512: ic * 1024 + half * 512 + 512],
                                 ic == 0, ic == NCH - 1,
                                 waits=([(s_dve, xnc_dve),
                                         (s_pe, ln_state["kP_war"][trot])] + bw(bank))
                                       if ic == 0 else (),
                                 inc=(s_pe.inc() if ic == NCH - 1 else None))
                        pe_n = s_pe.n
                        if is_k:
                            k.op("vector", lambda e, b=bank, h=half:
                                 e.scalar_tensor_tensor(u_s[h][:], pbk[b][:], 1.0,
                                                        bkb[:, h * 512:(h + 1) * 512],
                                                        ALU.mult, ALU.add),
                                 waits=[(s_pe, pe_n),
                                        (s_act, ln_state.get(f"u{half}_war", 0))],
                                 inc=s_dve.inc())
                            bank_war[bank] = (s_dve, s_dve.n)
                            k.op("vector", lambda e, h=half: e.tensor_scalar_min(m1_s[:], u_s[h][:], 0.0),
                                 waits=[(s_act, ln_state.get("m1_war", 0))],
                                 inc=s_dve.inc())
                            m1_n = s_dve.n
                            k.op("scalar", lambda e: e.activation(e_s[:], m1_s[:], AF.Exp,
                                                                  bias=0.0, scale=1.0),
                                 waits=[(s_dve, m1_n), (s_dve, ln_state.get("e_war", 0))],
                                 inc=s_act.inc())
                            ln_state["m1_war"] = s_act.n
                            k.op("scalar", lambda e, h=half: e.activation(r_s[:], u_s[h][:], AF.Relu,
                                                                          bias=0.0, scale=1.0),
                                 waits=[(s_dve, ln_state.get("r_war", 0))], inc=s_act.inc())
                            ln_state[f"u{half}_war"] = s_act.n
                            act_n = s_act.n
                            k.op("vector", lambda e, d=dstP, h=half:
                                 e.tensor_add(d[:, h * 512:(h + 1) * 512], e_s[:], r_s[:]),
                                 waits=[(s_act, act_n)], inc=s_dve.inc())
                            ln_state["e_war"] = s_dve.n
                            ln_state["r_war"] = s_dve.n
                        else:
                            k.op("vector", lambda e, b=bank, d=dstP, h=half:
                                 e.scalar_tensor_tensor(d[:, h * 512:(h + 1) * 512], pbk[b][:],
                                                        1.0, bvb[:, h * 512:(h + 1) * 512],
                                                        ALU.mult, ALU.add),
                                 waits=[(s_pe, pe_n)], inc=s_dve.inc())
                            bank_war[bank] = (s_dve, s_dve.n)
                ln_state["kv_in"][trot] = s_dve.n
                if tt > 0:
                    kv_mms(tt - 1)
            kv_mms(CH // 128 - 1)
            ln_state["xbuf_war"][c % 2] = s_pe.n

        for s in range(SEQS):
            if s == 0:
                fw0 = [(s_wl, WL0), (s_dve, WBUILD_DVE), (s_gp, GP_CONST)]
            else:
                fw0 = ([(s_pe, l0b_pe_end)]
                       + [(s_stx1[j], l0b_st_end[j]) for j in range(3)])
            stats = {}
            xn_dve = {}
            for c in range(NCHUNK):
                stats[c] = ln_front(x_d, s, c, first_waits=(fw0 if c == 0 else ()))
                xn_dve[c] = ln_back(xn0_d, s, c, stats[c])
                if c >= 1:
                    heavy_l0a(s, c - 1, xn_dve[c - 1])
                ln_store(xn0_d, s, c, xn_dve[c])
            heavy_l0a(s, NCHUNK - 1, xn_dve[NCHUNK - 1])
            k.op("vector", lambda e: e.tensor_copy(kv_bf[:], kv_acc[:]),
                 waits=[(s_pe, l0b_pe_end)], inc=s_dve.inc())
            kv_ready[s] = s_dve.n
            l0_xn_store[s] = s_stxn.n
            if kvdbg_d is not None:
                k.dma("scalar", kvdbg_d[s], kv_acc[:], waits=[(s_dve, kv_ready[s])],
                      inc=s_strow.inc(16))
            if "l0b" not in phases:
                continue
            st_b = {"m1B_act": [0, 0], "eB_dve": [0, 0], "rB_dve": [0, 0],
                    "gate_dve": [0, 0, 0, 0], "ga_pe": 0,
                    "qP_pe": s_pe.n,
                    "xn_pe": 0}
            ring = [0]
            def rbank():
                b = ring[0] % 8
                ring[0] += 1
                return b
            q_dve_oc = [0] * 8
            for c in range(NCHUNK):
                k.dma("sync", xncB[:], xn0_d[s, c],
                      waits=[(s_stxn, l0_xn_store[s]), (s_pe, st_b["xn_pe"]),
                             (s_dve, kv_ready[s])],
                      inc=s_xnl.inc(16))
                xn_ld = s_xnl.n
                for oc in range(8):
                    bank = rbank()
                    r2 = oc % 2
                    for ic in range(NCH):
                        k.mm(pbk[bank][:, :], wq[:, ic * 1024 + oc * 128: ic * 1024 + oc * 128 + 128],
                             xncB[:, ic * CH:(ic + 1) * CH], ic == 0, ic == NCH - 1,
                             waits=([(s_xnl, xn_ld), (s_dve, kv_ready[s])] + bw(bank)) if ic == 0 else (),
                             inc=(s_pe.inc() if ic == NCH - 1 else None))
                    pe_n = s_pe.n
                    k.op("vector", lambda e, b=bank, oc=oc, r2=r2:
                         e.tensor_scalar(m1B[r2][:], pbk[b][:], bq_c[:, oc:oc + 1], 0.0,
                                         ALU.add, ALU.min),
                         waits=[(s_pe, pe_n), (s_act, st_b["m1B_act"][r2])], inc=s_dve.inc())
                    m1_dve = s_dve.n
                    k.op("scalar", lambda e, r2=r2: e.activation(eB[r2][:], m1B[r2][:], AF.Exp,
                                                                 bias=0.0, scale=1.0),
                         waits=[(s_dve, m1_dve), (s_dve, st_b["eB_dve"][r2])], inc=s_act.inc())
                    st_b["m1B_act"][r2] = s_act.n
                    k.op("scalar", lambda e, b=bank, oc=oc, r2=r2:
                         e.activation(rB_[r2][:], pbk[b][:], AF.Relu, bias=bq_c[:, oc:oc + 1], scale=1.0),
                         waits=[(s_dve, st_b["rB_dve"][r2])], inc=s_act.inc())
                    act_n = s_act.n
                    bank_war[bank] = (s_act, act_n)
                    k.op("vector", lambda e, oc=oc, r2=r2:
                         e.tensor_add(qP[:, oc * CH:(oc + 1) * CH], eB[r2][:], rB_[r2][:]),
                         waits=[(s_act, act_n), (s_pe, st_b["qP_pe"])], inc=s_dve.inc())
                    st_b["eB_dve"][r2] = s_dve.n
                    st_b["rB_dve"][r2] = s_dve.n
                    q_dve_oc[oc] = s_dve.n
                q_dve = s_dve.n
                for oc in range(8):
                    gb = rbank()
                    g4 = oc % 4
                    for ic in range(NCH):
                        k.mm(pbk[gb][:, :], wg[:, ic * 1024 + oc * 128: ic * 1024 + oc * 128 + 128],
                             xncB[:, ic * CH:(ic + 1) * CH], ic == 0, ic == NCH - 1,
                             waits=([(s_dve, st_b["gate_dve"][g4])] + bw(gb)) if ic == 0 else (),
                             inc=(s_pe.inc() if ic == NCH - 1 else None))
                    pe_n = s_pe.n
                    k.op("scalar", lambda e, b=gb, oc=oc, g4=g4:
                         e.activation(gateB[g4][:], pbk[b][:], AF.Sigmoid,
                                      bias=bg_c[:, oc:oc + 1], scale=1.0),
                         waits=[(s_pe, pe_n)], inc=s_act.inc())
                    gate_act = s_act.n
                    bank_war[gb] = (s_act, gate_act)
                    ab = rbank()
                    for hh in range(2):
                        h = 2 * oc + hh
                        pb_ = 64 * (h % 2)
                        fb_ = 64 * (h // 2)
                        k.mm(pbk[ab][pb_:pb_ + 64, :],
                             kv_bf[pb_:pb_ + 64, fb_:fb_ + 64],
                             qP[pb_:pb_ + 64, oc * CH:(oc + 1) * CH],
                             True, True,
                             waits=([(s_dve, q_dve_oc[oc])] + bw(ab)) if hh == 0 else (),
                             inc=(s_pe.inc() if hh == 1 else None))
                    pe_n = s_pe.n
                    k.op("vector", lambda e, b=ab, oc=oc, g4=g4:
                         e.scalar_tensor_tensor(gaB[:, oc * CH:(oc + 1) * CH], pbk[b][:],
                                                1.0, gateB[g4][:], ALU.mult, ALU.mult),
                         waits=[(s_pe, pe_n), (s_act, gate_act), (s_pe, st_b["ga_pe"])],
                         inc=s_dve.inc())
                    st_b["gate_dve"][g4] = s_dve.n
                    bank_war[ab] = (s_dve, s_dve.n)
                ga_dve = s_dve.n
                for oc in range(8):
                    j3 = oc % 3
                    k.dma("sync", xslB[j3][:], x_d[s, oc * 128:(oc + 1) * 128, c * CH:(c + 1) * CH],
                          waits=[(s_stx1[j3], s_stx1[j3].n)], inc=s_xslB[j3].inc(16))
                    x_ld = s_xslB[j3].n
                    ob = rbank()
                    for ic in range(NCH):
                        k.mm(pbk[ob][:, :], wo[:, ic * 1024 + oc * 128: ic * 1024 + oc * 128 + 128],
                             gaB[:, ic * CH:(ic + 1) * CH], ic == 0, ic == NCH - 1,
                             waits=([(s_dve, ga_dve)] + bw(ob)) if ic == 0 else (),
                             inc=(s_pe.inc() if ic == NCH - 1 else None))
                    op_pe = s_pe.n
                    fb2 = rbank()
                    for ic in range(4):
                        if oc < 4:
                            lhs1 = wrT[:, (ic * 4 + oc) * 128:(ic * 4 + oc + 1) * 128]
                            lhs2 = wiTn[:, (ic * 4 + oc) * 128:(ic * 4 + oc + 1) * 128]
                        else:
                            lhs1 = wiT[:, (ic * 4 + oc - 4) * 128:(ic * 4 + oc - 4 + 1) * 128]
                            lhs2 = wrT[:, (ic * 4 + oc - 4) * 128:(ic * 4 + oc - 4 + 1) * 128]
                        k.mm(pbk[fb2][:, :], lhs1, xncB[:, ic * CH:(ic + 1) * CH],
                             ic == 0, False,
                             waits=bw(fb2) if ic == 0 else ())
                        k.mm(pbk[fb2][:, :], lhs2, xncB[:, (4 + ic) * CH:(4 + ic + 1) * CH],
                             False, ic == 3,
                             inc=(s_pe.inc() if ic == 3 else None))
                    ffn_pe = s_pe.n
                    k.op("vector", lambda e, j=j3, oc=oc, b=ob:
                         e.scalar_tensor_tensor(xslB[j][:], pbk[b][:], bo_c[:, oc:oc + 1],
                                                xslB[j][:], ALU.add, ALU.add),
                         waits=[(s_pe, op_pe), (s_xslB[j3], x_ld)], inc=s_dve.inc())
                    bank_war[ob] = (s_dve, s_dve.n)
                    k.op("vector", lambda e, j=j3, oc=oc, b=fb2:
                         e.scalar_tensor_tensor(xslB[j][:], pbk[b][:], bffn_c[:, oc:oc + 1],
                                                xslB[j][:], ALU.add, ALU.add),
                         waits=[(s_pe, ffn_pe)], inc=s_dve.inc())
                    bank_war[fb2] = (s_dve, s_dve.n)
                    k.dma("scalar", x1_d[s, oc * 128:(oc + 1) * 128, c * CH:(c + 1) * CH],
                          xslB[j3][:], waits=[(s_dve, s_dve.n)], inc=s_stx1[j3].inc(16))
                st_b["xn_pe"] = s_pe.n
                st_b["qP_pe"] = s_pe.n
                st_b["ga_pe"] = s_pe.n
            l0b_pe_end = s_pe.n
            l0b_st_end = [s_stx1[j].n for j in range(3)]
    L0_PE_END = s_pe.n
    L0_X1_ST = [s_stx1[j].n for j in range(3)]

    L1W_LD = 0
    if "l1a" in phases or "l1b" in phases or "l1c" in phases:
        for (dst, src) in ((l1wk, l1wk_i), (l1wv, l1wv_i), (l1wq, l1wq_i),
                           (l1wg, l1wg_i), (l1wo, l1wo_i)):
            k.dma("sync", dst[:], src[:], waits=[(s_pe, L0_PE_END)], inc=s_wfc.inc(16))
        L1W_LD = s_wfc.n
    if "l1c" in phases:
        k.dma("sync", wfc[:, :], l1wfc_i[:], waits=[(s_pe, L0_PE_END)], inc=s_wfc.inc(16))
        k.dma("sync", wproj[:, 8192:32768], l1wproj_i[:, 8192:32768], inc=s_wfc.inc(16))
    WFC_HI = s_wfc.n

    l1_xn_store = {}
    xo_store_n = 0
    b1_pe_end = {}
    if "l1a" in phases:
        vq_war = {"valsb": 0, "quersb": 0}
        b1_gp_end = 0
        b1_dve_end = 0
        pj_war_dve = 0
        def heavy_l1a(s, c, xnc_dve, gl_store):
            X = XB[c % 2]
            for tt in range(CH // 128):
                trot = tt % 2
                for ic in range(NCH):
                    k.mm(pbk[4][:, 0:256],
                         X[:, ic * CH + tt * 128: ic * CH + tt * 128 + 128],
                         l1wk[:, ic * 256:(ic + 1) * 256],
                         ic == 0, ic == NCH - 1,
                         waits=([(s_dve, xnc_dve), (s_wfc, L1W_LD)] + bw(4)) if ic == 0 else (),
                         inc=(s_pe.inc() if ic == NCH - 1 else None))
                pe_n = s_pe.n
                k.op("vector", lambda e, r=trot: e.tensor_copy(keyt[r][:], pbk[4][:, 0:256]),
                     waits=[(s_pe, pe_n), (s_stkey[trot], s_stkey[trot].n)],
                     inc=s_dve.inc())
                bank_war[4] = (s_dve, s_dve.n)
                k.dma("gpsimd", keys_d[s, c * 4 + tt], keyt[trot][:],
                      waits=[(s_dve, s_dve.n)], inc=s_stkey[trot].inc(16))
            for (dstT, wmat, scale) in ((valsb, l1wv, None), (quersb, l1wq, 0.0625)):
                for mc in range(2):
                    bank = 5 + mc
                    for ic in range(NCH):
                        k.mm(pbk[bank][:, :],
                             wmat[:, ic * 256 + mc * 128: ic * 256 + mc * 128 + 128],
                             X[:, ic * CH:(ic + 1) * CH],
                             ic == 0, ic == NCH - 1,
                             waits=([(s_dve, xnc_dve),
                                     (s_pe, vq_war["valsb" if dstT is valsb else "quersb"])]
                                    + bw(bank)) if ic == 0 else (),
                             inc=(s_pe.inc() if ic == NCH - 1 else None))
                    pe_n = s_pe.n
                    dslice = (mc * 2048 + c * CH, mc * 2048 + (c + 1) * CH)
                    if scale is None:
                        k.op("vector", lambda e, b=bank, d=dstT, ds=dslice:
                             e.tensor_copy(d[:, ds[0]:ds[1]], pbk[b][:]),
                             waits=[(s_pe, pe_n)], inc=s_dve.inc())
                    else:
                        k.op("vector", lambda e, b=bank, d=dstT, ds=dslice, sc=scale:
                             e.tensor_scalar_mul(d[:, ds[0]:ds[1]], pbk[b][:], sc),
                             waits=[(s_pe, pe_n)], inc=s_dve.inc())
                    bank_war[bank] = (s_dve, s_dve.n)
            for ic in range(NCH):
                k.mm(pbk[7][0:1, :], l1wg[:, ic:ic + 1], X[:, ic * CH:(ic + 1) * CH],
                     ic == 0, ic == NCH - 1,
                     waits=([(s_dve, xnc_dve)] + bw(7)) if ic == 0 else (),
                     inc=(s_pe.inc() if ic == NCH - 1 else None))
            pe_n = s_pe.n
            ln_state["xbuf_war"][c % 2] = pe_n
            k.op("scalar", lambda e: e.activation(pbk[7][32:33, 0:CH], pbk[7][0:1, :],
                                                  AF.Sigmoid, bias=l1bg[0:1, 0:1], scale=1.0),
                 waits=[(s_pe, pe_n)], inc=s_act.inc())
            rot = c % 2
            k.op("scalar", lambda e, r=rot: e.activation(glog_rowc[0:1, r * 512:(r + 1) * 512],
                                                         pbk[7][32:33, 0:CH], AF.Ln,
                                                         bias=1e-8, scale=1.0),
                 waits=[(s_stgl, gl_store.get(rot, 0))], inc=s_act.inc())
            bank_war[7] = (s_act, s_act.n)
            k.dma("scalar", glog_d[s, c * CH:(c + 1) * CH],
                  glog_rowc[0:1, rot * 512:(rot + 1) * 512],
                  waits=[(s_act, s_act.n)], inc=s_stgl.inc(16))
            gl_store[rot] = s_stgl.n

        for s in range(SEQS):
            gl_store = {}
            if s == 0:
                fw0 = ([(s_stx1[j], L0_X1_ST[j]) for j in range(3)]
                       + [(s_pe, L0_PE_END)])
            else:
                fw0 = ([(s_pe, s_pe.n), (s_gp, b1_gp_end), (s_dve, b1_dve_end)]
                       + [(s_stxo[j], s_stxo[j].n) for j in range(2)])
            stats = {}
            xn_dve = {}
            for c in range(NCHUNK):
                stats[c] = ln_front(x1_d, s, c, first_waits=(fw0 if c == 0 else ()))
                xn_dve[c] = ln_back(xn1_d, s, c, stats[c])
                if c >= 1:
                    heavy_l1a(s, c - 1, xn_dve[c - 1], gl_store)
                ln_store(xn1_d, s, c, xn_dve[c])
            heavy_l1a(s, NCHUNK - 1, xn_dve[NCHUNK - 1], gl_store)
            vq_dve = s_dve.n
            l1_xn_store[s] = s_stxn.n

            k.dma("sync", pj_a[:], glog_d[s].rearrange("(p j) -> p j", p=128),
                  waits=[(s_stgl, s_stgl.n), (s_dve, pj_war_dve)], inc=s_rowl.inc(16))
            k.op("vector", lambda e: e.tensor_tensor_scan(pj_b[:], pj_a[:], zeros16[:],
                                                          0.0, ALU.add, ALU.add),
                 waits=[(s_rowl, s_rowl.n), (s_gp, GP_CONST)], inc=s_dve.inc())
            scan_dve = s_dve.n
            k.mm(pbk[7][0:128, 256:257], U_tri[:], pj_b[:, 15:16], True, True,
                 waits=[(s_dve, scan_dve)] + bw(7), inc=s_pe.inc())
            pe_n = s_pe.n
            k.op("vector", lambda e: e.tensor_copy(prefcol[:], pbk[7][0:128, 256:257]),
                 waits=[(s_pe, pe_n)], inc=s_dve.inc())
            bank_war[7] = (s_dve, s_dve.n)
            k.op("vector", lambda e: e.tensor_scalar(cumtm[:], pj_b[:], prefcol[:, 0:1], 0.0,
                                                     ALU.add, ALU.add),
                 waits=[(s_gp, b1_gp_end), (s_dve, s_dve.n)], inc=s_dve.inc())
            pj_war_dve = s_dve.n
            k.dma("scalar", cum_d[s].rearrange("(p j) -> p j", p=128), cumtm[:],
                  waits=[(s_dve, s_dve.n)], inc=s_strow.inc(16))
            cum_st = s_strow.n
            if cumdbg_d is not None and s == 0:
                for (dsl, src) in (((0, 1), prefcol), ((1, 17), pj_b),
                                   ((17, 33), pj_a), ((33, 161), U_tri),
                                   ((161, 177), cumtm)):
                    k.op("scalar", lambda e, dsl=dsl, src=src: e.dma_start(
                         out=cumdbg_d[:, dsl[0]:dsl[1]], in_=src[:],
                         allow_slow_non_contiguous=True),
                         waits=[(s_dve, s_dve.n)], inc=s_strow.inc(16))
                cum_st = s_strow.n

            if "l1b" not in phases:
                continue
            cumj_dve = 0
            for jc in range(NCHUNK):
                k.dma("sync", cumrow[0:1, :], cum_d[s, jc * CH:(jc + 1) * CH],
                      waits=[(s_strow, cum_st), (s_pe, b1_pe_end.get("cumrow", 0))],
                      inc=s_rowl.inc(16))
                k.mm(pbk[6][:, :], ones_f32[:], cumrow[0:1, :], True, True,
                     waits=[(s_rowl, s_rowl.n)] + bw(6), inc=s_pe.inc())
                pe_n = s_pe.n
                b1_pe_end["cumrow"] = pe_n
                k.op("vector", lambda e, jc=jc: e.tensor_copy(cumjb[:, jc * CH:(jc + 1) * CH],
                                                              pbk[6][:]),
                     waits=[(s_pe, pe_n), (s_dve, cumj_dve)], inc=s_dve.inc())
                bank_war[6] = (s_dve, s_dve.n)
            cumj_dve = s_dve.n
            for it in range(16):
                k.dma("sync", cumtm[:, it:it + 1], cum_d[s, it * 128:(it + 1) * 128],
                      waits=[(s_dve, s_dve.n)], inc=s_rowl.inc(16))
            cumtm_ld = s_rowl.n
            k.wait("sync", s_strow, cum_st)

            b1 = {"wt_dve": [0, 0, 0, 0], "dscr_act": [0, 0, 0, 0],
                  "dec_dve": [0, 0, 0, 0], "reads_pe": 0,
                  "exp_done": [0, 0, 0, 0],
                  "x1c_pe": s_pe.n}
            for jc in range(NCHUNK):
                nit = 4 * jc + 4

                def reads_mms(it, nit=nit, s=s):
                    klrot = it % 2
                    krot = it % 4
                    wt_dve = b1["wt_dve"][krot]
                    key_ld = b1[f"key_ld{klrot}"]
                    for mc in range(2):
                        k.mm(pbk[2 + mc][:, :], keytB[klrot][:, mc * 128:(mc + 1) * 128],
                             wtb[krot][:], it == 0, it == nit - 1,
                             waits=([(s_dve, wt_dve), (s_keyl[klrot], key_ld)]
                                    + (bw(2 + mc) if it == 0 else [])),
                             inc=(s_pe.inc() if mc == 1 else None))
                    b1[f"keyB_pe{klrot}"] = s_pe.n
                    b1[f"wt_pe{krot}"] = s_pe.n

                sc_pe = {}
                for it in range(nit):
                    krot = it % 4
                    klrot = it % 2
                    k.dma("sync", keytB[klrot][:], keys_d[s, it],
                          waits=[(s_stkey[0], s_stkey[0].n), (s_stkey[1], s_stkey[1].n),
                                 (s_pe, b1.get(f"keyB_pe{klrot}", 0)),
                                 (s_dve, vq_dve)],
                          inc=s_keyl[klrot].inc(16))
                    b1[f"key_ld{klrot}"] = s_keyl[klrot].n
                    sb_ = (0, 1, 6, 7)[it % 4]
                    b1[f"sbank{krot}"] = sb_
                    for mc in range(2):
                        k.mm(pbk[sb_][:, :], valsb[:, mc * 2048 + it * 128: mc * 2048 + it * 128 + 128],
                             quersb[:, mc * 2048 + jc * CH: mc * 2048 + (jc + 1) * CH],
                             mc == 0, mc == 1,
                             waits=([(s_dve, vq_dve)] + bw(sb_)) if mc == 0 else (),
                             inc=(s_pe.inc() if mc == 1 else None))
                    sc_pe[it] = s_pe.n
                    k.op("vector", lambda e, r=krot, jc=jc, it=it:
                         e.tensor_scalar(dscr[r][:], cumjb[:, jc * CH:(jc + 1) * CH],
                                         cumtm[:, it:it + 1], 0.0, ALU.subtract, ALU.min),
                         waits=[(s_rowl, cumtm_ld), (s_dve, cumj_dve),
                                (s_dve, b1["dec_dve"][krot])],
                         inc=s_dve.inc())
                    if it % 2 == 0:
                        continue
                    ds_n = s_dve.n
                    r0 = krot - 1
                    k.op("scalar", lambda e, r0=r0:
                         e.activation(dscr2[r0 // 2][:], dscr2[r0 // 2][:],
                                      AF.Exp, bias=0.0, scale=1.0),
                         waits=[(s_dve, ds_n)], inc=s_act.inc())
                    dec_act = s_act.n
                    for px in (it - 1, it):
                        pr = px % 4
                        dec_ready = (s_act, dec_act)
                        if px >= 4 * jc:
                            k.op("gpsimd", lambda e, r=pr, jc=jc, px=px:
                                 e.affine_select(dscr[r][:], dscr[r][:], [[1, CH]], ALU.is_ge,
                                                 0.0, base=jc * CH - px * 128, channel_multiplier=-1),
                                 waits=[dec_ready], inc=s_gp.inc())
                            dec_ready = (s_gp, s_gp.n)
                        k.op("vector", lambda e, r=pr, b=b1[f"sbank{pr}"]:
                             e.scalar_tensor_tensor(wtb[r][:], pbk[b][:], 1.0, dscr[r][:],
                                                    ALU.mult, ALU.mult),
                             waits=[dec_ready, (s_pe, sc_pe[px]),
                                    (s_pe, b1.get(f"wt_pe{pr}", 0))],
                             inc=s_dve.inc())
                        b1["dec_dve"][pr] = s_dve.n
                        bank_war[b1[f"sbank{pr}"]] = (s_dve, s_dve.n)
                        b1["wt_dve"][pr] = s_dve.n
                    reads_mms(it - 1)
                    reads_mms(it)
                reads_pe = s_pe.n
                for mc in range(2):
                    k.op("vector", lambda e, mc=mc: e.tensor_scalar_mul(
                         readsb[:, mc * 512:(mc + 1) * 512], pbk[2 + mc][:], RSQ_T),
                         waits=[(s_pe, reads_pe)], inc=s_dve.inc())
                    bank_war[2 + mc] = (s_dve, s_dve.n)
                reads_dve = s_dve.n
                ob_banks = [4, 5, 2, 3, 0, 1, 6, 7]
                op_pe = {}
                k.dma("scalar", x1cB1[:], fm_ap(x1_d, s, jc * CH, CH),
                      waits=[(s_stxo[0], b1.get("xo_st", 0)), (s_dve, vq_dve),
                             (s_pe, b1.get("x1c_pe", 0))],
                      inc=s_x1l[0].inc(16))
                x1_ld = s_x1l[0].n
                for oc in range(8):
                    ob = ob_banks[oc]
                    for mc in range(2):
                        k.mm(pbk[ob][:, :], l1wo[:, mc * 1024 + oc * 128: mc * 1024 + oc * 128 + 128],
                             readsb[:, mc * 512:(mc + 1) * 512], mc == 0, mc == 1,
                             waits=([(s_dve, reads_dve)] + bw(ob)) if mc == 0 else (),
                             inc=(s_pe.inc() if mc == 1 else None))
                    op_pe[oc] = s_pe.n
                for oc in range(8):
                    ob = ob_banks[oc]
                    k.op("vector", lambda e, oc=oc, b=ob:
                         e.tensor_add(x1cB1[:, oc * CH:(oc + 1) * CH],
                                      x1cB1[:, oc * CH:(oc + 1) * CH], pbk[b][:]),
                         waits=[(s_pe, op_pe[oc]), (s_x1l[0], x1_ld)],
                         inc=s_dve.inc())
                    bank_war[ob] = (s_dve, s_dve.n)
                    for half in range(2):
                        kc = s * 8 + jc * 2 + half
                        k.dma("scalar", xo_d[kc][:, oc * 256:(oc + 1) * 256],
                              x1cB1[:, oc * CH + half * 256: oc * CH + half * 256 + 256],
                              waits=[(s_dve, s_dve.n)], inc=s_stxo[0].inc(16))
                b1["xo_st"] = s_stxo[0].n
            vq_war["valsb"] = s_pe.n
            vq_war["quersb"] = s_pe.n
            b1_gp_end = s_gp.n
            b1_dve_end = s_dve.n
    L1AB_PE = s_pe.n
    L1AB_GP = s_gp.n
    L1AB_DVE = s_dve.n
    XO_ST = [s_stxo[j].n for j in range(2)]
    XN1_ST = s_stxn.n

    if "l1c" in phases:
        k.dma("sync", wproj[:, 0:8192], l1wproj_i[:, 0:8192],
              waits=[(s_pe, L1AB_PE)], inc=s_wfc.inc(16))
        WFC_ALL = s_wfc.n
        cst = {"xn_pe": [0, 0], "y_dve": 0, "scan_dve": 0,
               "rr_dve": [0, 0], "spk_dve": 0, "acc_dve": 0, "y_pe": 0,
               "mem_gp": 0}
        for kk in range(NKC):
            s, t0 = kk // (T // CCH), (kk % (T // CCH)) * CCH
            xrot = kk % 2
            cbar = ([(s_pe, L1AB_PE), (s_gp, L1AB_GP), (s_dve, L1AB_DVE)]
                    if kk == 0 else [])
            k.dma("sync", xnC[xrot][:], xn1_d[kk],
                  waits=[(s_stxn, XN1_ST), (s_pe, cst["xn_pe"][xrot])] + cbar,
                  inc=s_xnlC.inc(16))
            xn_ld = s_xnlC.n
            for hp in range(16):
                fb_ = (0, 1, 6, 7)[hp % 4]
                for sub in range(2):
                    hf = 2 * hp + sub
                    for ic in range(NCH):
                        k.mm(pbk[fb_][:, sub * CCH:(sub + 1) * CCH],
                             wfc[:, ic * 4096 + hf * 128: ic * 4096 + hf * 128 + 128],
                             xnC[xrot][:, ic * CCH:(ic + 1) * CCH],
                             ic == 0, ic == NCH - 1,
                             waits=([(s_xnlC, xn_ld), (s_wfc, WFC_ALL)] + bw(fb_))
                                   if (ic == 0 and sub == 0) else (),
                             inc=(s_pe.inc() if (ic == NCH - 1 and sub == 1) else None))
                pe_n = s_pe.n
                k.op("scalar", lambda e, b=fb_: e.activation(rrC[0][:], pbk[b][:], AF.Relu,
                                                             bias=0.0, scale=1.0),
                     waits=[(s_pe, pe_n)], inc=s_act.inc())
                bank_war[fb_] = (s_act, s_act.n)
                k.op("scalar", lambda e, hp=hp:
                     e.activation(hC[:, hp * 512:(hp + 1) * 512], rrC[0][:], AF.Square,
                                  bias=0.0, scale=1.0),
                     inc=s_act.inc())
            h_gp = s_act.n
            cst["xn_pe"][xrot] = s_pe.n
            for oc in range(8):
                yb = 2 + oc // 2
                reg = (oc % 2) * CCH
                for hf in range(32):
                    k.mm(pbk[yb][:, reg:reg + CCH],
                         wproj[:, hf * 1024 + oc * 128: hf * 1024 + oc * 128 + 128],
                         hC[:, hf * CCH:(hf + 1) * CCH],
                         hf == 0, hf == 31,
                         waits=([(s_act, h_gp)] + (bw(yb) if oc % 2 == 0 else [])) if hf == 0 else (),
                         inc=(s_pe.inc() if hf == 31 else None))
            y_pe = s_pe.n
            cst["y_pe"] = y_pe
            for j in range(4):
                k.op("vector", lambda e, j=j: e.tensor_copy(yC[:, j * 512:(j + 1) * 512],
                                                            pbk[2 + j][:]),
                     waits=[(s_pe, y_pe)] + ([(s_dve, cst["scan_dve"])] if j == 0 else []),
                     inc=s_dve.inc())
                bank_war[2 + j] = (s_dve, s_dve.n)
            y_dve = s_dve.n
            k.dma("gpsimd", xoC[:], xo_d[kk],
                  waits=[(s_stxo[0], XO_ST[0]), (s_stxo[1], XO_ST[1]),
                         (s_sty, s_sty.n)], inc=s_xol.inc(16))
            xo_ld = s_xol.n
            k.op("vector", lambda e: e.tensor_single_scalar(spkC[:], yC[:], 1.0, ALU.is_gt))
            k.op("vector", lambda e: e.tensor_copy(accC[:], spkC[:]))
            for step in range(2, 6):
                if step == 2:
                    k.op("vector", lambda e: e.scalar_tensor_tensor(memC[:], yC[:], 0.9, yC[:],
                                                                    ALU.mult, ALU.add))
                else:
                    k.op("vector", lambda e: e.scalar_tensor_tensor(memC[:], memC[:], 0.9, yC[:],
                                                                    ALU.mult, ALU.add))
                k.op("vector", lambda e: e.tensor_sub(memC[:], memC[:], spkC[:]))
                k.op("vector", lambda e: e.tensor_single_scalar(spkC[:], memC[:], 1.0, ALU.is_gt))
                k.op("vector", lambda e: e.tensor_add(accC[:], accC[:], spkC[:]))
            k.op("vector", lambda e: e.scalar_tensor_tensor(xoC[:], accC[:], 0.2, xoC[:],
                                                            ALU.mult, ALU.add),
                 waits=[(s_xol, xo_ld)], inc=s_dve.inc())
            cst["scan_dve"] = s_dve.n
            cst["acc_dve"] = s_dve.n
            cst["spk_dve"] = s_dve.n
            k.dma("gpsimd", fm_ap(y_d, s, t0, CCH), xoC[:],
                  waits=[(s_dve, s_dve.n)], inc=s_sty.inc(16))

        k.wait("gpsimd", s_sty, s_sty.n)

    k.emit(ctx)
    nc._ctx = ctx
    return nc


_cache = {}


def kernel(**inputs):
    w = prep_weights(inputs)
    x = np.asarray(inputs["L0_attn_x"], np.float32)
    if "prog" not in _cache:
        _cache["prog"] = build_program()
    nc = _cache["prog"]
    in_maps = []
    for c in range(NCORES):
        xs = np.ascontiguousarray(x[c * SEQS:(c + 1) * SEQS].transpose(0, 2, 1))
        m = {"x": xs}
        m.update({kk: vv for kk, vv in w.items() if vv is not None})
        in_maps.append(m)
    res = run_bass_kernel_spmd(nc, in_maps, core_ids=list(range(NCORES)))
    outs = [r["y"].transpose(0, 2, 1) for r in res.results]
    return np.ascontiguousarray(np.concatenate(outs, axis=0))


# version 25
# speedup vs baseline: 1.0421x; 1.0007x over previous
import contextlib
import numpy as np
import ml_dtypes
import concourse.bass as bass
import concourse.mybir as mybir
from concourse.bass_utils import run_bass_kernel_spmd

F32 = mybir.dt.float32
BF16 = mybir.dt.bfloat16
AF = mybir.ActivationFunctionType
ALU = mybir.AluOpType
bf16 = ml_dtypes.bfloat16

NCORES = 8
SEQS = 4
T = 2048
D = 1024
NCH = 8
CH = 512
NCHUNK = T // CH
HEADS = 16
M1 = 256
HID = 4096
CCH = 256
NKC = SEQS * T // CCH
EPS = 1e-5
RSQ_T = 1.0 / float(np.sqrt(np.float32(T)))



def prep_weights(i):
    w = {}

    def wchunks(a, nout):
        a = np.asarray(a, np.float32)
        return np.ascontiguousarray(
            a.reshape(8, 128, nout).transpose(1, 0, 2).reshape(128, 8 * nout)).astype(bf16)

    for nm, key in (("wq", "L0_attn_Wq"), ("wk", "L0_attn_Wk"),
                    ("wv", "L0_attn_Wv"), ("wg", "L0_attn_Wg"), ("wo", "L0_attn_Wo")):
        w[nm] = wchunks(i[key], 1024)

    def bcol(b, n=8):
        return np.ascontiguousarray(np.asarray(b, np.float32).reshape(n, 128).T)

    w["bq_c"] = bcol(i["L0_attn_bq"])
    w["bg_c"] = bcol(i["L0_attn_bg"])
    w["bo_c"] = bcol(i["L0_attn_bo"])
    w["bffn_c"] = bcol(np.concatenate([np.asarray(i["L0_ffn_bias_real"], np.float32),
                                       np.asarray(i["L0_ffn_bias_imag"], np.float32)]))
    w["bk_r"] = np.asarray(i["L0_attn_bk"], np.float32).reshape(1, 1024).astype(bf16)
    w["bv_r"] = np.asarray(i["L0_attn_bv"], np.float32).reshape(1, 1024).astype(bf16)

    def g1m(g1):
        a = np.asarray(g1, np.float32)[0]
        return np.ascontiguousarray(a.transpose(1, 0, 2).reshape(4, 1024))

    def g2m(g2):
        a = np.asarray(g2, np.float32)[:, :, 0, :]
        return np.ascontiguousarray(a.transpose(0, 2, 1).reshape(4, 256))

    w["g1r"] = g1m(i["L0_ffn_G1_R"]); w["g2r"] = g2m(i["L0_ffn_G2_R"])
    w["g1i"] = g1m(i["L0_ffn_G1_I"]); w["g2i"] = g2m(i["L0_ffn_G2_I"])

    w["l1_wk"] = wchunks(np.asarray(i["L1_attn_Wk"], np.float32).T, 256)
    w["l1_wv"] = wchunks(np.asarray(i["L1_attn_Wv"], np.float32).T, 256)
    w["l1_wq"] = wchunks(np.asarray(i["L1_attn_Wq"], np.float32).T, 256)
    w["l1_wg"] = wchunks(np.asarray(i["L1_attn_Wg"], np.float32).T, 1)
    w["l1_bg"] = np.asarray(i["L1_attn_bg"], np.float32).reshape(1, 1)
    w["l1_wo"] = wchunks(np.asarray(i["L1_attn_Wo"], np.float32).T.reshape(2, 128, 1024)
                         .reshape(256, 1024), 1024)[:, :2048] if False else None
    a = np.asarray(i["L1_attn_Wo"], np.float32).T
    w["l1_wo"] = np.ascontiguousarray(
        a.reshape(2, 128, 1024).transpose(1, 0, 2).reshape(128, 2048)).astype(bf16)
    w["l1_wfc"] = wchunks(np.asarray(i["L1_ffn_W_fc"], np.float32).T, 4096)
    a = np.asarray(i["L1_ffn_W_proj"], np.float32).T
    w["l1_wproj"] = np.ascontiguousarray(
        a.reshape(32, 128, 1024).transpose(1, 0, 2).reshape(128, 32768)).astype(bf16)
    return w



class Sem:
    __slots__ = ("h", "n", "name")

    def __init__(self, h, name):
        self.h = h
        self.n = 0
        self.name = name

    def inc(self, k=1):
        self.n += k
        return (self, k)


class K:
    ENGINES = ("sync", "tensor", "vector", "scalar", "gpsimd")

    def __init__(self, nc):
        self.nc = nc
        self.ops = {e: [] for e in self.ENGINES}
        self._lastwait = {e: {} for e in self.ENGINES}

    def sem(self, ctx, name):
        return Sem(ctx.enter_context(self.nc.semaphore(name)), name)

    def wait(self, eng, sem, val):
        if sem is None or val <= 0:
            return
        lw = self._lastwait[eng]
        if lw.get(sem, -1) >= val:
            return
        lw[sem] = val
        self.ops[eng].append(("w", sem, val))

    def op(self, eng, build, waits=(), inc=None):
        for (s, v) in waits:
            self.wait(eng, s, v)
        incs = []
        if inc is not None:
            incs = [inc] if isinstance(inc, tuple) else [x for x in inc if x]
        self.ops[eng].append(("o", build, incs))

    def dma(self, queue, out, in_, waits=(), inc=None):
        self.op(queue, lambda e, o=out, i=in_: e.dma_start(out=o, in_=i), waits, inc)

    def mm(self, out, lhsT, rhs, start, stop, waits=(), inc=None):
        self.op("tensor",
                lambda e, o=out, l=lhsT, r=rhs, st=start, sp=stop:
                e.matmul(o, l, r, start=st, stop=sp),
                waits, inc)

    def emit(self, ctx):
        block = ctx.enter_context(self.nc.Block())
        for eng_name in self.ENGINES:
            items = self.ops[eng_name]
            if not items:
                continue

            def body(eng, items=items):
                for it in items:
                    if it[0] == "w":
                        eng.wait_ge(it[1].h, it[2])
                    else:
                        ins = it[1](eng)
                        for (s, kk) in it[2]:
                            ins = ins.then_inc(s.h, kk)
            getattr(block, eng_name)(body)



def build_program(debug=(), phases=("wbuild", "l0a", "l0b", "l1a", "l1b", "l1c")):
    nc = bass.Bass("TRN2", target_bir_lowering=False)
    ctx = contextlib.ExitStack()

    def kind(name):
        return "ExternalOutput" if name in debug else "Internal"

    x_d = nc.dram_tensor("x", [SEQS, D, T], F32, kind="ExternalInput")
    y_d = nc.dram_tensor("y", [SEQS, D, T], F32, kind="ExternalOutput")
    x1_d = nc.dram_tensor("x1", [SEQS, D, T], F32, kind=kind("x1"))
    xo_d = nc.dram_tensor("xo", [NKC, 128, 2048], F32, kind=kind("xo"))
    xn0_d = nc.dram_tensor("xn0", [SEQS, NCHUNK, 128, 4096], BF16, kind=kind("xn0"))
    xn1_d = nc.dram_tensor("xn1", [NKC, 128, 2048], BF16, kind=kind("xn1"))
    keys_d = nc.dram_tensor("keysd", [SEQS, 16, 128, M1], BF16, kind=kind("keysd"))
    glog_d = nc.dram_tensor("glogd", [SEQS, T], F32, kind=kind("glogd"))
    cum_d = nc.dram_tensor("cumd", [SEQS, T], F32, kind=kind("cumd"))
    t1r_d = nc.dram_tensor("t1r", [D, 256], BF16, kind=kind("t1r"))
    t1i_d = nc.dram_tensor("t1i", [D, 256], BF16, kind=kind("t1i"))
    kvdbg_d = (nc.dram_tensor("kvdbg", [SEQS, 128, 512], F32, kind="ExternalOutput")
               if "kvdbg" in debug else None)
    cumdbg_d = (nc.dram_tensor("cumdbg", [128, 177], F32, kind="ExternalOutput")
                if "cumdbg" in debug else None)

    def din(name, shape, dt):
        return nc.dram_tensor(name, shape, dt, kind="ExternalInput")

    wq_i = din("wq", [128, 8192], BF16); wk_i = din("wk", [128, 8192], BF16)
    wv_i = din("wv", [128, 8192], BF16); wg_i = din("wg", [128, 8192], BF16)
    wo_i = din("wo", [128, 8192], BF16)
    bq_i = din("bq_c", [128, 8], F32); bg_i = din("bg_c", [128, 8], F32)
    bo_i = din("bo_c", [128, 8], F32); bffn_i = din("bffn_c", [128, 8], F32)
    bk_i = din("bk_r", [1, 1024], BF16); bv_i = din("bv_r", [1, 1024], BF16)
    g1r_i = din("g1r", [4, 1024], F32); g2r_i = din("g2r", [4, 256], F32)
    g1i_i = din("g1i", [4, 1024], F32); g2i_i = din("g2i", [4, 256], F32)
    l1wk_i = din("l1_wk", [128, 2048], BF16); l1wv_i = din("l1_wv", [128, 2048], BF16)
    l1wq_i = din("l1_wq", [128, 2048], BF16); l1wg_i = din("l1_wg", [128, 8], BF16)
    l1bg_i = din("l1_bg", [1, 1], F32)
    l1wo_i = din("l1_wo", [128, 2048], BF16)
    l1wfc_i = din("l1_wfc", [128, 32768], BF16)
    l1wproj_i = din("l1_wproj", [128, 32768], BF16)

    AW_BYTES = 131072
    AA_BYTES = 61440
    aw_t = ctx.enter_context(nc.sbuf_tensor("arena_w", [128, AW_BYTES], mybir.dt.uint8))
    aa_t = ctx.enter_context(nc.sbuf_tensor("arena_a", [128, AA_BYTES], mybir.dt.uint8))
    aw_base = nc.lookup_mloc(aw_t).addr
    aa_base = nc.lookup_mloc(aa_t).addr

    def at_w(name, shape, dt, off):
        per = int(np.prod(shape[1:])) * mybir.dt.size(dt)
        assert off + per <= AW_BYTES, (name, off, per)
        return nc.alloc_sbuf_tensor_at(name, shape, dt, offset=aw_base + off)

    def at_a(name, shape, dt, off):
        per = int(np.prod(shape[1:])) * mybir.dt.size(dt)
        assert off + per <= AA_BYTES, (name, off, per)
        return nc.alloc_sbuf_tensor_at(name, shape, dt, offset=aa_base + off)

    wq = at_w("wq_s", [128, 8192], BF16, 0)
    wk = at_w("wk_s", [128, 8192], BF16, 16384)
    wv = at_w("wv_s", [128, 8192], BF16, 32768)
    wg = at_w("wg_s", [128, 8192], BF16, 49152)
    wo = at_w("wo_s", [128, 8192], BF16, 65536)
    wrT = at_w("wrT", [128, 2048], BF16, 81920)
    wiT = at_w("wiT", [128, 2048], BF16, 86016)
    wiTn = at_w("wiTn", [128, 2048], BF16, 90112)
    g1r = at_w("g1r_s", [4, 1024], F32, 94208)
    g2r = at_w("g2r_s", [4, 256], F32, 98304)
    g1i = at_w("g1i_s", [4, 1024], F32, 99328)
    g2i = at_w("g2i_s", [4, 256], F32, 103424)
    bkb = at_w("bkb", [128, 1024], BF16, 104448)
    bvb = at_w("bvb", [128, 1024], BF16, 106496)
    tmpw = at_w("tmpw", [128, 256], BF16, 108544)
    bk_r = at_w("bk_rs", [1, 1024], BF16, 109056)
    bv_r = at_w("bv_rs", [1, 1024], BF16, 111104)
    wfc = at_w("wfc_s", [128, 32768], BF16, 0)
    wproj = at_w("wproj_s", [128, 32768], BF16, 65536)
    l1wk = at_w("l1wk_s", [128, 2048], BF16, 65536)
    l1wv = at_w("l1wv_s", [128, 2048], BF16, 69632)
    l1wq = at_w("l1wq_s", [128, 2048], BF16, 73728)
    l1wo = at_w("l1wo_s", [128, 2048], BF16, 77824)
    def sbst(name, shape, dt):
        return ctx.enter_context(nc.sbuf_tensor(name, shape, dt))

    l1wg = sbst("l1wg_s", [128, 8], BF16)

    l1bg = sbst("l1bg_s", [1, 1], F32)
    bq_c = sbst("bq_cs", [128, 8], F32)
    bg_c = sbst("bg_cs", [128, 8], F32)
    bo_c = sbst("bo_cs", [128, 8], F32)
    bffn_c = sbst("bffn_cs", [128, 8], F32)
    ones_bf = sbst("ones_bf", [1, 128], BF16)
    ones_f32 = sbst("ones_f32", [1, 128], F32)
    ones_col = sbst("ones_col", [128, 1], BF16)
    U_tri = sbst("U_tri", [128, 128], F32)
    kv_acc = sbst("kv_acc", [128, 512], F32)
    kv_bf = sbst("kv_bf", [128, 512], BF16)
    m_rowc = sbst("m_rowc", [1, 1024], BF16)
    rstd_rowc = sbst("rstd_rowc", [1, 1024], F32)
    glog_rowc = sbst("glog_rowc", [1, 1024], F32)
    cumrow = sbst("cumrow", [1, 512], F32)
    ln_row = sbst("ln_row", [1, 512], F32)
    prefcol = sbst("prefcol", [128, 1], F32)
    pj_a = sbst("pj_a", [128, 16], F32)
    pj_b = sbst("pj_b", [128, 16], F32)
    cumtm = sbst("cumtm", [128, 16], F32)
    zeros16 = sbst("zeros16", [128, 16], F32)
    c_eps = sbst("c_eps", [128, 1], F32)
    c_eps8 = sbst("c_eps8", [128, 1], F32)
    nc.const_aps.register(F32, EPS, c_eps[:, :])
    nc.const_aps.register(F32, 1e-8, c_eps8[:, :])

    xhalf = at_a("xhalf", [128, 2048], F32, 0)
    xb = at_a("xb", [128, 4096], BF16, 8192)
    xnc = at_a("xnc", [128, 4096], BF16, 16384)
    mb_s = at_a("mb_s", [128, 512], BF16, 24576)
    xsq = at_a("xsq", [128, 4096], BF16, 47104)
    xsq_st = at_a("xsq_st", [128, 2048], F32, 47104)
    rb_s = at_a("rb_s", [128, 512], F32, 55296)
    u_s = [at_a(f"u_s{j}", [128, 512], F32, 25600 + 2048 * j) for j in range(2)]
    m1_s = at_a("m1_s", [128, 512], F32, 29696)
    e_s = at_a("e_s", [128, 512], F32, 31744)
    r_s = at_a("r_s", [128, 512], F32, 33792)
    kP = [at_a(f"kP{j}", [128, 1024], BF16, 35840 + 2048 * j) for j in range(2)]
    vP = [at_a(f"vP{j}", [128, 1024], BF16, 39936 + 2048 * j) for j in range(2)]
    valsb = at_a("valsb", [128, 4096], BF16, 25600)
    quersb = at_a("quersb", [128, 4096], BF16, 33792)
    keyt = [at_a(f"keyt{j}", [128, 256], BF16, 44032 + 512 * j) for j in range(2)]
    xncB = at_a("xncB", [128, 4096], BF16, 0)
    xslB = [at_a(f"xslB{j}", [128, 512], F32, 8192 + 2048 * j) for j in range(3)]
    qP = at_a("qP", [128, 4096], BF16, 14336)
    gateB = [at_a(f"gateB{j}", [128, 512], F32, 22528 + 2048 * j) for j in range(4)]
    gaB = at_a("gaB", [128, 4096], BF16, 30720)
    m1B = [at_a(f"m1B{j}", [128, 512], F32, 38912 + 2048 * j) for j in range(2)]
    eB = [at_a(f"eB{j}", [128, 512], F32, 43008 + 2048 * j) for j in range(2)]
    rB_ = [at_a(f"rB_{j}", [128, 512], F32, 47104 + 2048 * j) for j in range(2)]
    cumjb = at_a("cumjb", [128, 2048], F32, 0)
    x1cB1 = at_a("x1cB1", [128, 4096], F32, 8192)
    keytB = [at_a(f"keytB{j}", [128, 256], BF16, 24576 + 512 * j) for j in range(2)]
    dscr = [at_a(f"dscr{j}", [128, 512], F32, 41984 + 2048 * j) for j in range(4)]
    dscr2 = [at_a(f"dscr2_{j}", [128, 1024], F32, 41984 + 4096 * j) for j in range(2)]
    wtb = [at_a(f"wtb{j}", [128, 512], BF16, 50176 + 1024 * j) for j in range(4)]
    readsb = at_a("readsb", [128, 1024], BF16, 54272)
    xnC = [at_a(f"xnC{j}", [128, 2048], BF16, 4096 * j) for j in range(2)]
    hC = at_a("hC", [128, 8192], BF16, 8192)
    yC = at_a("yC", [128, 2048], F32, 24576)
    memC = at_a("memC", [128, 2048], F32, 32768)
    spkC = at_a("spkC", [128, 2048], BF16, 40960)
    accC = at_a("accC", [128, 2048], BF16, 45056)
    xoC = at_a("xoC", [128, 2048], F32, 49152)
    rrC = [at_a(f"rrC{j}", [128, 512], F32, 57344 + 2048 * j) for j in range(2)]

    pbk = [ctx.enter_context(nc.psum_tensor(f"pb{j}", [128, 512], F32)) for j in range(8)]

    k = K(nc)
    s_pe = k.sem(ctx, "s_pe")
    s_dve = k.sem(ctx, "s_dve")
    s_act = k.sem(ctx, "s_act")
    s_gp = k.sem(ctx, "s_gp")
    s_wl = k.sem(ctx, "s_wl")
    s_xsl = [k.sem(ctx, f"s_xsl{j}") for j in range(2)]
    s_xcp = k.sem(ctx, "s_xcp")
    s_xnl = k.sem(ctx, "s_xnl")
    s_xslB = [k.sem(ctx, f"s_xslB{j}") for j in range(3)]
    s_stxn = k.sem(ctx, "s_stxn")
    s_stx1 = [k.sem(ctx, f"s_stx1{j}") for j in range(3)]
    s_stkey = [k.sem(ctx, f"s_stkey{j}") for j in range(2)]
    s_stgl = k.sem(ctx, "s_stgl")
    s_strow = k.sem(ctx, "s_strow")
    s_stt1 = k.sem(ctx, "s_stt1")
    s_keyl = [k.sem(ctx, f"s_keyl{j}") for j in range(2)]
    s_x1l = [k.sem(ctx, f"s_x1l{j}") for j in range(4)]
    s_rowl = k.sem(ctx, "s_rowl")
    s_wfc = k.sem(ctx, "s_wfc")
    s_xnlC = k.sem(ctx, "s_xnlC")
    s_xol = k.sem(ctx, "s_xol")
    s_sty = k.sem(ctx, "s_sty")
    s_stxo = [k.sem(ctx, f"s_stxo{j}") for j in range(2)]

    def fm_ap(dram, s, t0, tlen):
        return dram[s, :, t0:t0 + tlen].rearrange("(ic p) t -> p ic t", p=128)

    for (dst, src) in ((wq, wq_i), (wk, wk_i), (wv, wv_i), (wg, wg_i), (wo, wo_i),
                       (g1r, g1r_i), (g2r, g2r_i), (g1i, g1i_i), (g2i, g2i_i),
                       (bk_r, bk_i), (bv_r, bv_i), (l1bg, l1bg_i),
                       (bq_c, bq_i), (bg_c, bg_i), (bo_c, bo_i), (bffn_c, bffn_i)):
        k.dma("sync", dst[:], src[:], inc=s_wl.inc(16))
    WL0 = s_wl.n

    for t, v in ((ones_bf, 1.0), (ones_f32, 1.0), (ones_col, 1.0), (U_tri, 1.0),
                 (zeros16, 0.0), (c_eps, EPS), (c_eps8, 1e-8)):
        k.op("gpsimd", lambda e, t=t, v=v: e.memset(t[:], v), inc=s_gp.inc())
    k.op("gpsimd", lambda e: e.affine_select(U_tri[:], U_tri[:], [[1, 128]], ALU.is_gt,
                                             0.0, base=0, channel_multiplier=-1),
         waits=[(s_gp, s_gp.n)], inc=s_gp.inc())
    GP_CONST = s_gp.n

    WBUILD_DVE = 0
    if "wbuild" in phases and "skip_bcast" not in debug:
        for half in range(2):
            k.mm(pbk[6][:, :], ones_bf[:], bk_r[0:1, half * 512:(half + 1) * 512],
                 True, True, waits=[(s_wl, WL0), (s_gp, GP_CONST), (s_dve, s_dve.n)])
            k.mm(pbk[7][:, :], ones_bf[:], bv_r[0:1, half * 512:(half + 1) * 512],
                 True, True, inc=s_pe.inc())
            pe_n = s_pe.n
            k.op("vector", lambda e, h=half: e.tensor_copy(bkb[:, h * 512:(h + 1) * 512], pbk[6][:]),
                 waits=[(s_pe, pe_n)])
            k.op("vector", lambda e, h=half: e.tensor_copy(bvb[:, h * 512:(h + 1) * 512], pbk[7][:]),
                 inc=s_dve.inc())
    if "wbuild" in phases and "skip_t1" not in debug:
        for (g1_s, g2_s, t1_t) in ((g1r, g2r, t1r_d), (g1i, g2i, t1i_d)):
            for oc in range(8):
                k.mm(pbk[5][:, 0:256], g1_s[:, oc * 128:(oc + 1) * 128], g2_s[:],
                     True, True,
                     waits=[(s_wl, WL0), (s_dve, s_dve.n), (s_stt1, s_stt1.n)],
                     inc=s_pe.inc())
                pe_n = s_pe.n
                k.op("vector", lambda e: e.tensor_copy(tmpw[:], pbk[5][:, 0:256]),
                     waits=[(s_pe, pe_n)], inc=s_dve.inc())
                k.dma("scalar", t1_t[oc * 128:(oc + 1) * 128, :], tmpw[:],
                      waits=[(s_dve, s_dve.n)], inc=s_stt1.inc(16))
        T1_DONE = s_stt1.n
        for (t1_t, w_t) in ((t1r_d, wrT), (t1i_d, wiT)):
            rr_ap = t1_t.ap().rearrange("(b d) (g e) -> d g b e", b=32, d=32, g=16, e=16)
            for ic in range(4):
                for oc in range(4):
                    k.dma("sync", w_t[:, (ic * 4 + oc) * 128:(ic * 4 + oc + 1) * 128],
                          rr_ap[ic * 8:(ic + 1) * 8, :, oc * 8:(oc + 1) * 8, :],
                          waits=[(s_stt1, T1_DONE)], inc=s_wl.inc(16))
        WT_DONE = s_wl.n
        k.op("vector", lambda e: e.tensor_scalar_mul(wiTn[:], wiT[:], -1.0),
             waits=[(s_wl, WT_DONE)], inc=s_dve.inc())
        WBUILD_DVE = s_dve.n

    XB = [xb, xnc]
    ln_state = {"cast_war": 0, "stats_pe": 0, "pb0_act": 0, "pb0_dve": 0,
                "pb1_act": 0, "pb2_dve": 0, "pb3_dve": 0,
                "mrow_pe": 0, "rrow_pe": 0, "xbuf_war": [0, 0], "sq_pe": 0,
                "kv_in": [0, 0], "kP_war": [0, 0]}

    def ln_front(src_d, s, c, first_waits=()):
        X = XB[c % 2]
        war0 = ln_state["cast_war"]
        for h in range(2):
            stg = xhalf if h == 0 else xsq_st
            lw = [(s_xcp, war0)] + list(first_waits)
            if h == 1:
                lw.append((s_pe, ln_state["sq_pe"]))
            k.dma("sync", stg[:],
                  src_d[s, h * 512:(h + 1) * 512, c * CH:(c + 1) * CH]
                  .rearrange("(ic p) t -> p ic t", p=128),
                  waits=lw, inc=s_xsl[0].inc(16))
            k.op("vector",
                 lambda e, h=h, X=X, stg=stg:
                 e.tensor_copy(X[:, h * 2048:(h + 1) * 2048], stg[:]),
                 waits=[(s_xsl[0], s_xsl[0].n), (s_pe, ln_state["xbuf_war"][c % 2])],
                 inc=s_xcp.inc())
        ln_state["cast_war"] = s_xcp.n
        cast_done = s_xcp.n
        for ic in range(NCH):
            k.mm(pbk[0][0:1, :], ones_col[:], X[:, ic * CH:(ic + 1) * CH],
                 ic == 0, ic == NCH - 1,
                 waits=([(s_xcp, cast_done), (s_act, ln_state["pb0_act"]),
                         (s_dve, ln_state["pb0_dve"])] if ic == 0 else ()))
        k.op("scalar",
             lambda e, X=X: e.activation(xsq[:], X[:], AF.Square, bias=0.0, scale=1.0),
             waits=[(s_xcp, cast_done), (s_pe, ln_state["sq_pe"])], inc=s_act.inc())
        sq_act = s_act.n
        for ic in range(NCH):
            k.mm(pbk[0][32:33, :], ones_col[:], xsq[:, ic * CH:(ic + 1) * CH],
                 ic == 0, ic == NCH - 1,
                 waits=([(s_act, sq_act)] if ic == 0 else ()),
                 inc=(s_pe.inc() if ic == NCH - 1 else None))
        ln_state["stats_pe"] = s_pe.n
        ln_state["sq_pe"] = s_pe.n
        return s_pe.n

    def ln_back(xn_dst_d, s, c, stats_done):
        X = XB[c % 2]
        rot = c % 2
        k.op("scalar", lambda e: e.activation(ln_row[0:1, :], pbk[0][0:1, :], AF.Square,
                                              bias=0.0, scale=1.0 / 1024),
             waits=[(s_pe, stats_done)], inc=s_act.inc())
        k.op("vector", lambda e: e.scalar_tensor_tensor(pbk[1][0:1, 0:CH], pbk[0][32:33, :],
                                                        1.0 / 1024, ln_row[0:1, :],
                                                        ALU.mult, ALU.subtract),
             waits=[(s_act, s_act.n), (s_act, ln_state["pb1_act"]), (s_pe, stats_done)],
             inc=s_dve.inc())
        k.op("scalar", lambda e: e.activation(pbk[1][64:65, 0:CH], pbk[1][0:1, 0:CH], AF.Ln,
                                              bias=EPS, scale=1.0),
             waits=[(s_dve, s_dve.n)], inc=s_act.inc())
        k.op("scalar", lambda e, r=rot: e.activation(rstd_rowc[0:1, r * 512:(r + 1) * 512],
                                                     pbk[1][64:65, 0:CH], AF.Exp,
                                                     bias=0.0, scale=-0.5),
             waits=[(s_pe, ln_state["rrow_pe"])], inc=s_act.inc())
        rows_act = s_act.n
        ln_state["pb1_act"] = rows_act
        k.op("vector", lambda e, r=rot: e.tensor_scalar_mul(m_rowc[0:1, r * 512:(r + 1) * 512],
                                                            pbk[0][0:1, :], 1.0 / 1024),
             waits=[(s_pe, stats_done), (s_pe, ln_state["mrow_pe"])], inc=s_dve.inc())
        rows_dve = s_dve.n
        ln_state["pb0_act"] = rows_act
        ln_state["pb0_dve"] = rows_dve
        k.mm(pbk[2][:, :], ones_bf[:], m_rowc[0:1, rot * 512:(rot + 1) * 512], True, True,
             waits=[(s_dve, rows_dve), (s_dve, ln_state["pb2_dve"])])
        ln_state["mrow_pe"] = s_pe.n + 1
        k.mm(pbk[3][:, :], ones_f32[:], rstd_rowc[0:1, rot * 512:(rot + 1) * 512], True, True,
             waits=[(s_act, rows_act), (s_dve, ln_state["pb3_dve"])], inc=s_pe.inc())
        ln_state["rrow_pe"] = s_pe.n
        bc_pe = s_pe.n
        k.op("vector", lambda e: e.tensor_copy(mb_s[:], pbk[2][:]),
             waits=[(s_pe, bc_pe)], inc=s_dve.inc())
        ln_state["pb2_dve"] = s_dve.n
        k.op("vector", lambda e: e.tensor_copy(rb_s[:], pbk[3][:]), inc=s_dve.inc())
        ln_state["pb3_dve"] = s_dve.n
        mbrb_dve = s_dve.n
        X_v = X[:].rearrange("p (r t) -> p r t", r=NCH)
        mb_b = bass.AP(mb_s, 0, [[512, 128], [0, NCH], [1, CH]])
        rb_b = bass.AP(rb_s, 0, [[512, 128], [0, NCH], [1, CH]])
        k.op("vector", lambda e, X_v=X_v: e.tensor_tensor(X_v, X_v, mb_b, ALU.subtract),
             waits=[(s_dve, mbrb_dve)], inc=None)
        k.op("vector", lambda e, X_v=X_v: e.tensor_tensor(X_v, X_v, rb_b, ALU.mult),
             inc=s_dve.inc())
        xnc_dve = s_dve.n
        return xnc_dve

    def ln_store(xn_dst_d, s, c, xnc_dve):
        X = XB[c % 2]
        if xn_dst_d is xn0_d:
            k.dma("scalar", xn0_d[s, c], X[:], waits=[(s_dve, xnc_dve)],
                  inc=s_stxn.inc(16))
        elif xn_dst_d is xn1_d:
            xv = X[:].rearrange("p (ic t) -> p ic t", ic=NCH)
            for half in range(2):
                kc = s * 8 + c * 2 + half
                k.dma("scalar", xn1_d[kc].rearrange("p (ic t) -> p ic t", ic=NCH),
                      xv[:, :, half * 256:(half + 1) * 256],
                      waits=[(s_dve, xnc_dve)], inc=s_stxn.inc(16))

    kv_ready = {}
    l0_xn_store = {}
    bank_war = {j: (None, 0) for j in range(8)}

    def bw(j):
        sem, n = bank_war[j]
        return [(sem, n)] if sem else []

    l0b_pe_end = 0
    l0b_st_end = [0, 0, 0]
    if "l0a" in phases or "l0b" in phases:

        def heavy_l0a(s, c, xnc_dve):
            X = XB[c % 2]

            def kv_mms(tt):
                trot = tt % 2
                kv_in = ln_state["kv_in"][trot]
                tile_idx = c * (CH // 128) + tt
                bank = 6 + tile_idx % 2
                for h in range(HEADS):
                    pb_, fb_ = 64 * (h % 2), 64 * (h // 2)
                    k.mm(pbk[bank][pb_:pb_ + 64, fb_:fb_ + 64],
                         kP[trot][:, 64 * h: 64 * h + 64],
                         vP[trot][:, 64 * h: 64 * h + 64],
                         True, True,
                         waits=([(s_dve, kv_in)] + bw(bank)) if h == 0 else (),
                         inc=(s_pe.inc() if h == HEADS - 1 else None))
                pe_n = s_pe.n
                if tile_idx == 0:
                    k.op("vector", lambda e, b=bank: e.tensor_copy(kv_acc[:], pbk[b][:]),
                         waits=[(s_pe, pe_n), (s_pe, l0b_pe_end)], inc=s_dve.inc())
                else:
                    k.op("vector", lambda e, b=bank: e.tensor_add(kv_acc[:], kv_acc[:], pbk[b][:]),
                         waits=[(s_pe, pe_n)], inc=s_dve.inc())
                bank_war[bank] = (s_dve, s_dve.n)
                ln_state["kP_war"][trot] = s_pe.n

            for tt in range(CH // 128):
                trot = tt % 2
                for (wmat, dstP, bias_b, is_k) in ((wk, kP[trot], bkb, True),
                                                   (wv, vP[trot], bvb, False)):
                    for half in range(2):
                        bank = 4 + half
                        for ic in range(NCH):
                            k.mm(pbk[bank][:, :],
                                 X[:, ic * CH + tt * 128: ic * CH + tt * 128 + 128],
                                 wmat[:, ic * 1024 + half * 512: ic * 1024 + half * 512 + 512],
                                 ic == 0, ic == NCH - 1,
                                 waits=([(s_dve, xnc_dve),
                                         (s_pe, ln_state["kP_war"][trot])] + bw(bank))
                                       if ic == 0 else (),
                                 inc=(s_pe.inc() if ic == NCH - 1 else None))
                        pe_n = s_pe.n
                        if is_k:
                            k.op("vector", lambda e, b=bank, h=half:
                                 e.scalar_tensor_tensor(u_s[h][:], pbk[b][:], 1.0,
                                                        bkb[:, h * 512:(h + 1) * 512],
                                                        ALU.mult, ALU.add),
                                 waits=[(s_pe, pe_n),
                                        (s_act, ln_state.get(f"u{half}_war", 0))],
                                 inc=s_dve.inc())
                            bank_war[bank] = (s_dve, s_dve.n)
                            k.op("vector", lambda e, h=half: e.tensor_scalar_min(m1_s[:], u_s[h][:], 0.0),
                                 waits=[(s_act, ln_state.get("m1_war", 0))],
                                 inc=s_dve.inc())
                            m1_n = s_dve.n
                            k.op("scalar", lambda e: e.activation(e_s[:], m1_s[:], AF.Exp,
                                                                  bias=0.0, scale=1.0),
                                 waits=[(s_dve, m1_n), (s_dve, ln_state.get("e_war", 0))],
                                 inc=s_act.inc())
                            ln_state["m1_war"] = s_act.n
                            k.op("scalar", lambda e, h=half: e.activation(r_s[:], u_s[h][:], AF.Relu,
                                                                          bias=0.0, scale=1.0),
                                 waits=[(s_dve, ln_state.get("r_war", 0))], inc=s_act.inc())
                            ln_state[f"u{half}_war"] = s_act.n
                            act_n = s_act.n
                            k.op("vector", lambda e, d=dstP, h=half:
                                 e.tensor_add(d[:, h * 512:(h + 1) * 512], e_s[:], r_s[:]),
                                 waits=[(s_act, act_n)], inc=s_dve.inc())
                            ln_state["e_war"] = s_dve.n
                            ln_state["r_war"] = s_dve.n
                        else:
                            k.op("vector", lambda e, b=bank, d=dstP, h=half:
                                 e.scalar_tensor_tensor(d[:, h * 512:(h + 1) * 512], pbk[b][:],
                                                        1.0, bvb[:, h * 512:(h + 1) * 512],
                                                        ALU.mult, ALU.add),
                                 waits=[(s_pe, pe_n)], inc=s_dve.inc())
                            bank_war[bank] = (s_dve, s_dve.n)
                ln_state["kv_in"][trot] = s_dve.n
                if tt > 0:
                    kv_mms(tt - 1)
            kv_mms(CH // 128 - 1)
            ln_state["xbuf_war"][c % 2] = s_pe.n

        for s in range(SEQS):
            if s == 0:
                fw0 = [(s_wl, WL0), (s_dve, WBUILD_DVE), (s_gp, GP_CONST)]
            else:
                fw0 = ([(s_pe, l0b_pe_end)]
                       + [(s_stx1[j], l0b_st_end[j]) for j in range(3)])
            stats = {}
            xn_dve = {}
            for c in range(NCHUNK):
                stats[c] = ln_front(x_d, s, c, first_waits=(fw0 if c == 0 else ()))
                xn_dve[c] = ln_back(xn0_d, s, c, stats[c])
                if c >= 1:
                    heavy_l0a(s, c - 1, xn_dve[c - 1])
                ln_store(xn0_d, s, c, xn_dve[c])
            heavy_l0a(s, NCHUNK - 1, xn_dve[NCHUNK - 1])
            k.op("vector", lambda e: e.tensor_copy(kv_bf[:], kv_acc[:]),
                 waits=[(s_pe, l0b_pe_end)], inc=s_dve.inc())
            kv_ready[s] = s_dve.n
            l0_xn_store[s] = s_stxn.n
            if kvdbg_d is not None:
                k.dma("scalar", kvdbg_d[s], kv_acc[:], waits=[(s_dve, kv_ready[s])],
                      inc=s_strow.inc(16))
            if "l0b" not in phases:
                continue
            st_b = {"m1B_act": [0, 0], "eB_dve": [0, 0], "rB_dve": [0, 0],
                    "gate_dve": [0, 0, 0, 0], "ga_pe": 0,
                    "qP_pe": s_pe.n,
                    "xn_pe": 0}
            ring = [0]
            def rbank():
                b = ring[0] % 8
                ring[0] += 1
                return b
            q_dve_oc = [0] * 8
            for c in range(NCHUNK):
                k.dma("sync", xncB[:], xn0_d[s, c],
                      waits=[(s_stxn, l0_xn_store[s]), (s_pe, st_b["xn_pe"]),
                             (s_dve, kv_ready[s])],
                      inc=s_xnl.inc(16))
                xn_ld = s_xnl.n
                for oc in range(8):
                    bank = rbank()
                    r2 = oc % 2
                    for ic in range(NCH):
                        k.mm(pbk[bank][:, :], wq[:, ic * 1024 + oc * 128: ic * 1024 + oc * 128 + 128],
                             xncB[:, ic * CH:(ic + 1) * CH], ic == 0, ic == NCH - 1,
                             waits=([(s_xnl, xn_ld), (s_dve, kv_ready[s])] + bw(bank)) if ic == 0 else (),
                             inc=(s_pe.inc() if ic == NCH - 1 else None))
                    pe_n = s_pe.n
                    k.op("vector", lambda e, b=bank, oc=oc, r2=r2:
                         e.tensor_scalar(m1B[r2][:], pbk[b][:], bq_c[:, oc:oc + 1], 0.0,
                                         ALU.add, ALU.min),
                         waits=[(s_pe, pe_n), (s_act, st_b["m1B_act"][r2])], inc=s_dve.inc())
                    m1_dve = s_dve.n
                    k.op("scalar", lambda e, r2=r2: e.activation(eB[r2][:], m1B[r2][:], AF.Exp,
                                                                 bias=0.0, scale=1.0),
                         waits=[(s_dve, m1_dve), (s_dve, st_b["eB_dve"][r2])], inc=s_act.inc())
                    st_b["m1B_act"][r2] = s_act.n
                    k.op("scalar", lambda e, b=bank, oc=oc, r2=r2:
                         e.activation(rB_[r2][:], pbk[b][:], AF.Relu, bias=bq_c[:, oc:oc + 1], scale=1.0),
                         waits=[(s_dve, st_b["rB_dve"][r2])], inc=s_act.inc())
                    act_n = s_act.n
                    bank_war[bank] = (s_act, act_n)
                    k.op("vector", lambda e, oc=oc, r2=r2:
                         e.tensor_add(qP[:, oc * CH:(oc + 1) * CH], eB[r2][:], rB_[r2][:]),
                         waits=[(s_act, act_n), (s_pe, st_b["qP_pe"])], inc=s_dve.inc())
                    st_b["eB_dve"][r2] = s_dve.n
                    st_b["rB_dve"][r2] = s_dve.n
                    q_dve_oc[oc] = s_dve.n
                q_dve = s_dve.n
                for oc in range(8):
                    gb = rbank()
                    g4 = oc % 4
                    for ic in range(NCH):
                        k.mm(pbk[gb][:, :], wg[:, ic * 1024 + oc * 128: ic * 1024 + oc * 128 + 128],
                             xncB[:, ic * CH:(ic + 1) * CH], ic == 0, ic == NCH - 1,
                             waits=([(s_dve, st_b["gate_dve"][g4])] + bw(gb)) if ic == 0 else (),
                             inc=(s_pe.inc() if ic == NCH - 1 else None))
                    pe_n = s_pe.n
                    k.op("scalar", lambda e, b=gb, oc=oc, g4=g4:
                         e.activation(gateB[g4][:], pbk[b][:], AF.Sigmoid,
                                      bias=bg_c[:, oc:oc + 1], scale=1.0),
                         waits=[(s_pe, pe_n)], inc=s_act.inc())
                    gate_act = s_act.n
                    bank_war[gb] = (s_act, gate_act)
                    ab = rbank()
                    for hh in range(2):
                        h = 2 * oc + hh
                        pb_ = 64 * (h % 2)
                        fb_ = 64 * (h // 2)
                        k.mm(pbk[ab][pb_:pb_ + 64, :],
                             kv_bf[pb_:pb_ + 64, fb_:fb_ + 64],
                             qP[pb_:pb_ + 64, oc * CH:(oc + 1) * CH],
                             True, True,
                             waits=([(s_dve, q_dve_oc[oc])] + bw(ab)) if hh == 0 else (),
                             inc=(s_pe.inc() if hh == 1 else None))
                    pe_n = s_pe.n
                    k.op("vector", lambda e, b=ab, oc=oc, g4=g4:
                         e.scalar_tensor_tensor(gaB[:, oc * CH:(oc + 1) * CH], pbk[b][:],
                                                1.0, gateB[g4][:], ALU.mult, ALU.mult),
                         waits=[(s_pe, pe_n), (s_act, gate_act), (s_pe, st_b["ga_pe"])],
                         inc=s_dve.inc())
                    st_b["gate_dve"][g4] = s_dve.n
                    bank_war[ab] = (s_dve, s_dve.n)
                ga_dve = s_dve.n
                for oc in range(8):
                    j3 = oc % 3
                    k.dma("sync", xslB[j3][:], x_d[s, oc * 128:(oc + 1) * 128, c * CH:(c + 1) * CH],
                          waits=[(s_stx1[j3], s_stx1[j3].n)], inc=s_xslB[j3].inc(16))
                    x_ld = s_xslB[j3].n
                    ob = rbank()
                    for ic in range(NCH):
                        k.mm(pbk[ob][:, :], wo[:, ic * 1024 + oc * 128: ic * 1024 + oc * 128 + 128],
                             gaB[:, ic * CH:(ic + 1) * CH], ic == 0, ic == NCH - 1,
                             waits=([(s_dve, ga_dve)] + bw(ob)) if ic == 0 else (),
                             inc=(s_pe.inc() if ic == NCH - 1 else None))
                    op_pe = s_pe.n
                    fb2 = rbank()
                    for ic in range(4):
                        if oc < 4:
                            lhs1 = wrT[:, (ic * 4 + oc) * 128:(ic * 4 + oc + 1) * 128]
                            lhs2 = wiTn[:, (ic * 4 + oc) * 128:(ic * 4 + oc + 1) * 128]
                        else:
                            lhs1 = wiT[:, (ic * 4 + oc - 4) * 128:(ic * 4 + oc - 4 + 1) * 128]
                            lhs2 = wrT[:, (ic * 4 + oc - 4) * 128:(ic * 4 + oc - 4 + 1) * 128]
                        k.mm(pbk[fb2][:, :], lhs1, xncB[:, ic * CH:(ic + 1) * CH],
                             ic == 0, False,
                             waits=bw(fb2) if ic == 0 else ())
                        k.mm(pbk[fb2][:, :], lhs2, xncB[:, (4 + ic) * CH:(4 + ic + 1) * CH],
                             False, ic == 3,
                             inc=(s_pe.inc() if ic == 3 else None))
                    ffn_pe = s_pe.n
                    k.op("vector", lambda e, j=j3, oc=oc, b=ob:
                         e.scalar_tensor_tensor(xslB[j][:], pbk[b][:], bo_c[:, oc:oc + 1],
                                                xslB[j][:], ALU.add, ALU.add),
                         waits=[(s_pe, op_pe), (s_xslB[j3], x_ld)], inc=s_dve.inc())
                    bank_war[ob] = (s_dve, s_dve.n)
                    k.op("vector", lambda e, j=j3, oc=oc, b=fb2:
                         e.scalar_tensor_tensor(xslB[j][:], pbk[b][:], bffn_c[:, oc:oc + 1],
                                                xslB[j][:], ALU.add, ALU.add),
                         waits=[(s_pe, ffn_pe)], inc=s_dve.inc())
                    bank_war[fb2] = (s_dve, s_dve.n)
                    k.dma("scalar", x1_d[s, oc * 128:(oc + 1) * 128, c * CH:(c + 1) * CH],
                          xslB[j3][:], waits=[(s_dve, s_dve.n)], inc=s_stx1[j3].inc(16))
                st_b["xn_pe"] = s_pe.n
                st_b["qP_pe"] = s_pe.n
                st_b["ga_pe"] = s_pe.n
            l0b_pe_end = s_pe.n
            l0b_st_end = [s_stx1[j].n for j in range(3)]
    L0_PE_END = s_pe.n
    L0_X1_ST = [s_stx1[j].n for j in range(3)]

    L1W_LD = 0
    if "l1a" in phases or "l1b" in phases or "l1c" in phases:
        for (dst, src) in ((l1wk, l1wk_i), (l1wv, l1wv_i), (l1wq, l1wq_i),
                           (l1wg, l1wg_i), (l1wo, l1wo_i)):
            k.dma("sync", dst[:], src[:], waits=[(s_pe, L0_PE_END)], inc=s_wfc.inc(16))
        L1W_LD = s_wfc.n
    if "l1c" in phases:
        k.dma("sync", wfc[:, :], l1wfc_i[:], waits=[(s_pe, L0_PE_END)], inc=s_wfc.inc(16))
        k.dma("sync", wproj[:, 8192:32768], l1wproj_i[:, 8192:32768], inc=s_wfc.inc(16))
    WFC_HI = s_wfc.n

    l1_xn_store = {}
    xo_store_n = 0
    b1_pe_end = {}
    if "l1a" in phases:
        vq_war = {"valsb": 0, "quersb": 0}
        b1_gp_end = 0
        b1_dve_end = 0
        pj_war_dve = 0
        def heavy_l1a(s, c, xnc_dve, gl_store):
            X = XB[c % 2]
            for tt in range(CH // 128):
                trot = tt % 2
                for ic in range(NCH):
                    k.mm(pbk[4][:, 0:256],
                         X[:, ic * CH + tt * 128: ic * CH + tt * 128 + 128],
                         l1wk[:, ic * 256:(ic + 1) * 256],
                         ic == 0, ic == NCH - 1,
                         waits=([(s_dve, xnc_dve), (s_wfc, L1W_LD)] + bw(4)) if ic == 0 else (),
                         inc=(s_pe.inc() if ic == NCH - 1 else None))
                pe_n = s_pe.n
                k.op("vector", lambda e, r=trot: e.tensor_copy(keyt[r][:], pbk[4][:, 0:256]),
                     waits=[(s_pe, pe_n), (s_stkey[trot], s_stkey[trot].n)],
                     inc=s_dve.inc())
                bank_war[4] = (s_dve, s_dve.n)
                k.dma("scalar", keys_d[s, c * 4 + tt], keyt[trot][:],
                      waits=[(s_dve, s_dve.n)], inc=s_stkey[trot].inc(16))
            for (dstT, wmat, scale) in ((valsb, l1wv, None), (quersb, l1wq, 0.0625)):
                for mc in range(2):
                    bank = 5 + mc
                    for ic in range(NCH):
                        k.mm(pbk[bank][:, :],
                             wmat[:, ic * 256 + mc * 128: ic * 256 + mc * 128 + 128],
                             X[:, ic * CH:(ic + 1) * CH],
                             ic == 0, ic == NCH - 1,
                             waits=([(s_dve, xnc_dve),
                                     (s_pe, vq_war["valsb" if dstT is valsb else "quersb"])]
                                    + bw(bank)) if ic == 0 else (),
                             inc=(s_pe.inc() if ic == NCH - 1 else None))
                    pe_n = s_pe.n
                    dslice = (mc * 2048 + c * CH, mc * 2048 + (c + 1) * CH)
                    if scale is None:
                        k.op("vector", lambda e, b=bank, d=dstT, ds=dslice:
                             e.tensor_copy(d[:, ds[0]:ds[1]], pbk[b][:]),
                             waits=[(s_pe, pe_n)], inc=s_dve.inc())
                    else:
                        k.op("vector", lambda e, b=bank, d=dstT, ds=dslice, sc=scale:
                             e.tensor_scalar_mul(d[:, ds[0]:ds[1]], pbk[b][:], sc),
                             waits=[(s_pe, pe_n)], inc=s_dve.inc())
                    bank_war[bank] = (s_dve, s_dve.n)
            for ic in range(NCH):
                k.mm(pbk[7][0:1, :], l1wg[:, ic:ic + 1], X[:, ic * CH:(ic + 1) * CH],
                     ic == 0, ic == NCH - 1,
                     waits=([(s_dve, xnc_dve)] + bw(7)) if ic == 0 else (),
                     inc=(s_pe.inc() if ic == NCH - 1 else None))
            pe_n = s_pe.n
            ln_state["xbuf_war"][c % 2] = pe_n
            k.op("scalar", lambda e: e.activation(pbk[7][32:33, 0:CH], pbk[7][0:1, :],
                                                  AF.Sigmoid, bias=l1bg[0:1, 0:1], scale=1.0),
                 waits=[(s_pe, pe_n)], inc=s_act.inc())
            rot = c % 2
            k.op("scalar", lambda e, r=rot: e.activation(glog_rowc[0:1, r * 512:(r + 1) * 512],
                                                         pbk[7][32:33, 0:CH], AF.Ln,
                                                         bias=1e-8, scale=1.0),
                 waits=[(s_stgl, gl_store.get(rot, 0))], inc=s_act.inc())
            bank_war[7] = (s_act, s_act.n)
            k.dma("scalar", glog_d[s, c * CH:(c + 1) * CH],
                  glog_rowc[0:1, rot * 512:(rot + 1) * 512],
                  waits=[(s_act, s_act.n)], inc=s_stgl.inc(16))
            gl_store[rot] = s_stgl.n

        for s in range(SEQS):
            gl_store = {}
            if s == 0:
                fw0 = ([(s_stx1[j], L0_X1_ST[j]) for j in range(3)]
                       + [(s_pe, L0_PE_END)])
            else:
                fw0 = ([(s_pe, s_pe.n), (s_gp, b1_gp_end), (s_dve, b1_dve_end)]
                       + [(s_stxo[j], s_stxo[j].n) for j in range(2)])
            stats = {}
            xn_dve = {}
            for c in range(NCHUNK):
                stats[c] = ln_front(x1_d, s, c, first_waits=(fw0 if c == 0 else ()))
                xn_dve[c] = ln_back(xn1_d, s, c, stats[c])
                if c >= 1:
                    heavy_l1a(s, c - 1, xn_dve[c - 1], gl_store)
                ln_store(xn1_d, s, c, xn_dve[c])
            heavy_l1a(s, NCHUNK - 1, xn_dve[NCHUNK - 1], gl_store)
            vq_dve = s_dve.n
            l1_xn_store[s] = s_stxn.n

            k.dma("sync", pj_a[:], glog_d[s].rearrange("(p j) -> p j", p=128),
                  waits=[(s_stgl, s_stgl.n), (s_dve, pj_war_dve)], inc=s_rowl.inc(16))
            k.op("vector", lambda e: e.tensor_tensor_scan(pj_b[:], pj_a[:], zeros16[:],
                                                          0.0, ALU.add, ALU.add),
                 waits=[(s_rowl, s_rowl.n), (s_gp, GP_CONST)], inc=s_dve.inc())
            scan_dve = s_dve.n
            k.mm(pbk[7][0:128, 256:257], U_tri[:], pj_b[:, 15:16], True, True,
                 waits=[(s_dve, scan_dve)] + bw(7), inc=s_pe.inc())
            pe_n = s_pe.n
            k.op("vector", lambda e: e.tensor_copy(prefcol[:], pbk[7][0:128, 256:257]),
                 waits=[(s_pe, pe_n)], inc=s_dve.inc())
            bank_war[7] = (s_dve, s_dve.n)
            k.op("vector", lambda e: e.tensor_scalar(cumtm[:], pj_b[:], prefcol[:, 0:1], 0.0,
                                                     ALU.add, ALU.add),
                 waits=[(s_gp, b1_gp_end), (s_dve, s_dve.n)], inc=s_dve.inc())
            pj_war_dve = s_dve.n
            k.dma("scalar", cum_d[s].rearrange("(p j) -> p j", p=128), cumtm[:],
                  waits=[(s_dve, s_dve.n)], inc=s_strow.inc(16))
            cum_st = s_strow.n
            if cumdbg_d is not None and s == 0:
                for (dsl, src) in (((0, 1), prefcol), ((1, 17), pj_b),
                                   ((17, 33), pj_a), ((33, 161), U_tri),
                                   ((161, 177), cumtm)):
                    k.op("scalar", lambda e, dsl=dsl, src=src: e.dma_start(
                         out=cumdbg_d[:, dsl[0]:dsl[1]], in_=src[:],
                         allow_slow_non_contiguous=True),
                         waits=[(s_dve, s_dve.n)], inc=s_strow.inc(16))
                cum_st = s_strow.n

            if "l1b" not in phases:
                continue
            cumj_dve = 0
            for jc in range(NCHUNK):
                k.dma("sync", cumrow[0:1, :], cum_d[s, jc * CH:(jc + 1) * CH],
                      waits=[(s_strow, cum_st), (s_pe, b1_pe_end.get("cumrow", 0))],
                      inc=s_rowl.inc(16))
                k.mm(pbk[6][:, :], ones_f32[:], cumrow[0:1, :], True, True,
                     waits=[(s_rowl, s_rowl.n)] + bw(6), inc=s_pe.inc())
                pe_n = s_pe.n
                b1_pe_end["cumrow"] = pe_n
                k.op("vector", lambda e, jc=jc: e.tensor_copy(cumjb[:, jc * CH:(jc + 1) * CH],
                                                              pbk[6][:]),
                     waits=[(s_pe, pe_n), (s_dve, cumj_dve)], inc=s_dve.inc())
                bank_war[6] = (s_dve, s_dve.n)
            cumj_dve = s_dve.n
            for it in range(16):
                k.dma("sync", cumtm[:, it:it + 1], cum_d[s, it * 128:(it + 1) * 128],
                      waits=[(s_dve, s_dve.n)], inc=s_rowl.inc(16))
            cumtm_ld = s_rowl.n
            k.wait("sync", s_strow, cum_st)

            b1 = {"wt_dve": [0, 0, 0, 0], "dscr_act": [0, 0, 0, 0],
                  "dec_dve": [0, 0, 0, 0], "reads_pe": 0,
                  "exp_done": [0, 0, 0, 0],
                  "x1c_pe": s_pe.n}
            for jc in range(NCHUNK):
                nit = 4 * jc + 4

                def reads_mms(it, nit=nit, s=s):
                    klrot = it % 2
                    krot = it % 4
                    wt_dve = b1["wt_dve"][krot]
                    key_ld = b1[f"key_ld{klrot}"]
                    for mc in range(2):
                        k.mm(pbk[2 + mc][:, :], keytB[klrot][:, mc * 128:(mc + 1) * 128],
                             wtb[krot][:], it == 0, it == nit - 1,
                             waits=([(s_dve, wt_dve), (s_keyl[klrot], key_ld)]
                                    + (bw(2 + mc) if it == 0 else [])),
                             inc=(s_pe.inc() if mc == 1 else None))
                    b1[f"keyB_pe{klrot}"] = s_pe.n
                    b1[f"wt_pe{krot}"] = s_pe.n

                sc_pe = {}
                for it in range(nit):
                    krot = it % 4
                    klrot = it % 2
                    k.dma("sync", keytB[klrot][:], keys_d[s, it],
                          waits=[(s_stkey[0], s_stkey[0].n), (s_stkey[1], s_stkey[1].n),
                                 (s_pe, b1.get(f"keyB_pe{klrot}", 0)),
                                 (s_dve, vq_dve)],
                          inc=s_keyl[klrot].inc(16))
                    b1[f"key_ld{klrot}"] = s_keyl[klrot].n
                    sb_ = (0, 1, 6, 7)[it % 4]
                    b1[f"sbank{krot}"] = sb_
                    for mc in range(2):
                        k.mm(pbk[sb_][:, :], valsb[:, mc * 2048 + it * 128: mc * 2048 + it * 128 + 128],
                             quersb[:, mc * 2048 + jc * CH: mc * 2048 + (jc + 1) * CH],
                             mc == 0, mc == 1,
                             waits=([(s_dve, vq_dve)] + bw(sb_)) if mc == 0 else (),
                             inc=(s_pe.inc() if mc == 1 else None))
                    sc_pe[it] = s_pe.n
                    k.op("vector", lambda e, r=krot, jc=jc, it=it:
                         e.tensor_scalar(dscr[r][:], cumjb[:, jc * CH:(jc + 1) * CH],
                                         cumtm[:, it:it + 1], 0.0, ALU.subtract, ALU.min),
                         waits=[(s_rowl, cumtm_ld), (s_dve, cumj_dve),
                                (s_dve, b1["dec_dve"][krot])],
                         inc=s_dve.inc())
                    if it % 2 == 0:
                        continue
                    ds_n = s_dve.n
                    r0 = krot - 1
                    k.op("scalar", lambda e, r0=r0:
                         e.activation(dscr2[r0 // 2][:], dscr2[r0 // 2][:],
                                      AF.Exp, bias=0.0, scale=1.0),
                         waits=[(s_dve, ds_n)], inc=s_act.inc())
                    dec_act = s_act.n
                    for px in (it - 1, it):
                        pr = px % 4
                        dec_ready = (s_act, dec_act)
                        if px >= 4 * jc:
                            k.op("gpsimd", lambda e, r=pr, jc=jc, px=px:
                                 e.affine_select(dscr[r][:], dscr[r][:], [[1, CH]], ALU.is_ge,
                                                 0.0, base=jc * CH - px * 128, channel_multiplier=-1),
                                 waits=[dec_ready], inc=s_gp.inc())
                            dec_ready = (s_gp, s_gp.n)
                        k.op("vector", lambda e, r=pr, b=b1[f"sbank{pr}"]:
                             e.scalar_tensor_tensor(wtb[r][:], pbk[b][:], 1.0, dscr[r][:],
                                                    ALU.mult, ALU.mult),
                             waits=[dec_ready, (s_pe, sc_pe[px]),
                                    (s_pe, b1.get(f"wt_pe{pr}", 0))],
                             inc=s_dve.inc())
                        b1["dec_dve"][pr] = s_dve.n
                        bank_war[b1[f"sbank{pr}"]] = (s_dve, s_dve.n)
                        b1["wt_dve"][pr] = s_dve.n
                    reads_mms(it - 1)
                    reads_mms(it)
                reads_pe = s_pe.n
                for mc in range(2):
                    k.op("vector", lambda e, mc=mc: e.tensor_scalar_mul(
                         readsb[:, mc * 512:(mc + 1) * 512], pbk[2 + mc][:], RSQ_T),
                         waits=[(s_pe, reads_pe)], inc=s_dve.inc())
                    bank_war[2 + mc] = (s_dve, s_dve.n)
                reads_dve = s_dve.n
                ob_banks = [4, 5, 2, 3, 0, 1, 6, 7]
                op_pe = {}
                k.dma("scalar", x1cB1[:], fm_ap(x1_d, s, jc * CH, CH),
                      waits=[(s_stxo[0], b1.get("xo_st", 0)), (s_dve, vq_dve),
                             (s_pe, b1.get("x1c_pe", 0))],
                      inc=s_x1l[0].inc(16))
                x1_ld = s_x1l[0].n
                for oc in range(8):
                    ob = ob_banks[oc]
                    for mc in range(2):
                        k.mm(pbk[ob][:, :], l1wo[:, mc * 1024 + oc * 128: mc * 1024 + oc * 128 + 128],
                             readsb[:, mc * 512:(mc + 1) * 512], mc == 0, mc == 1,
                             waits=([(s_dve, reads_dve)] + bw(ob)) if mc == 0 else (),
                             inc=(s_pe.inc() if mc == 1 else None))
                    op_pe[oc] = s_pe.n
                for oc in range(8):
                    ob = ob_banks[oc]
                    k.op("vector", lambda e, oc=oc, b=ob:
                         e.tensor_add(x1cB1[:, oc * CH:(oc + 1) * CH],
                                      x1cB1[:, oc * CH:(oc + 1) * CH], pbk[b][:]),
                         waits=[(s_pe, op_pe[oc]), (s_x1l[0], x1_ld)],
                         inc=s_dve.inc())
                    bank_war[ob] = (s_dve, s_dve.n)
                    for half in range(2):
                        kc = s * 8 + jc * 2 + half
                        k.dma("scalar", xo_d[kc][:, oc * 256:(oc + 1) * 256],
                              x1cB1[:, oc * CH + half * 256: oc * CH + half * 256 + 256],
                              waits=[(s_dve, s_dve.n)], inc=s_stxo[0].inc(16))
                b1["xo_st"] = s_stxo[0].n
            vq_war["valsb"] = s_pe.n
            vq_war["quersb"] = s_pe.n
            b1_gp_end = s_gp.n
            b1_dve_end = s_dve.n
    L1AB_PE = s_pe.n
    L1AB_GP = s_gp.n
    L1AB_DVE = s_dve.n
    XO_ST = [s_stxo[j].n for j in range(2)]
    XN1_ST = s_stxn.n

    if "l1c" in phases:
        k.dma("sync", wproj[:, 0:8192], l1wproj_i[:, 0:8192],
              waits=[(s_pe, L1AB_PE)], inc=s_wfc.inc(16))
        WFC_ALL = s_wfc.n
        cst = {"xn_pe": [0, 0], "y_dve": 0, "scan_dve": 0,
               "rr_dve": [0, 0], "spk_dve": 0, "acc_dve": 0, "y_pe": 0,
               "mem_gp": 0}
        for kk in range(NKC):
            s, t0 = kk // (T // CCH), (kk % (T // CCH)) * CCH
            xrot = kk % 2
            cbar = ([(s_pe, L1AB_PE), (s_gp, L1AB_GP), (s_dve, L1AB_DVE)]
                    if kk == 0 else [])
            k.dma("sync", xnC[xrot][:], xn1_d[kk],
                  waits=[(s_stxn, XN1_ST), (s_pe, cst["xn_pe"][xrot])] + cbar,
                  inc=s_xnlC.inc(16))
            xn_ld = s_xnlC.n
            for hp in range(16):
                fb_ = (0, 1, 6, 7)[hp % 4]
                for sub in range(2):
                    hf = 2 * hp + sub
                    for ic in range(NCH):
                        k.mm(pbk[fb_][:, sub * CCH:(sub + 1) * CCH],
                             wfc[:, ic * 4096 + hf * 128: ic * 4096 + hf * 128 + 128],
                             xnC[xrot][:, ic * CCH:(ic + 1) * CCH],
                             ic == 0, ic == NCH - 1,
                             waits=([(s_xnlC, xn_ld), (s_wfc, WFC_ALL)] + bw(fb_))
                                   if (ic == 0 and sub == 0) else (),
                             inc=(s_pe.inc() if (ic == NCH - 1 and sub == 1) else None))
                pe_n = s_pe.n
                k.op("scalar", lambda e, b=fb_: e.activation(rrC[0][:], pbk[b][:], AF.Relu,
                                                             bias=0.0, scale=1.0),
                     waits=[(s_pe, pe_n)], inc=s_act.inc())
                bank_war[fb_] = (s_act, s_act.n)
                k.op("scalar", lambda e, hp=hp:
                     e.activation(hC[:, hp * 512:(hp + 1) * 512], rrC[0][:], AF.Square,
                                  bias=0.0, scale=1.0),
                     inc=s_act.inc())
            h_gp = s_act.n
            cst["xn_pe"][xrot] = s_pe.n
            for oc in range(8):
                yb = 2 + oc // 2
                reg = (oc % 2) * CCH
                for hf in range(32):
                    k.mm(pbk[yb][:, reg:reg + CCH],
                         wproj[:, hf * 1024 + oc * 128: hf * 1024 + oc * 128 + 128],
                         hC[:, hf * CCH:(hf + 1) * CCH],
                         hf == 0, hf == 31,
                         waits=([(s_act, h_gp)] + (bw(yb) if oc % 2 == 0 else [])) if hf == 0 else (),
                         inc=(s_pe.inc() if hf == 31 else None))
            y_pe = s_pe.n
            cst["y_pe"] = y_pe
            for j in range(4):
                k.op("vector", lambda e, j=j: e.tensor_copy(yC[:, j * 512:(j + 1) * 512],
                                                            pbk[2 + j][:]),
                     waits=[(s_pe, y_pe)] + ([(s_dve, cst["scan_dve"])] if j == 0 else []),
                     inc=s_dve.inc())
                bank_war[2 + j] = (s_dve, s_dve.n)
            y_dve = s_dve.n
            k.dma("gpsimd", xoC[:], xo_d[kk],
                  waits=[(s_stxo[0], XO_ST[0]), (s_stxo[1], XO_ST[1]),
                         (s_sty, s_sty.n)], inc=s_xol.inc(16))
            xo_ld = s_xol.n
            k.op("vector", lambda e: e.tensor_single_scalar(spkC[:], yC[:], 1.0, ALU.is_gt))
            k.op("vector", lambda e: e.tensor_copy(accC[:], spkC[:]))
            for step in range(2, 6):
                if step == 2:
                    k.op("vector", lambda e: e.scalar_tensor_tensor(memC[:], yC[:], 0.9, yC[:],
                                                                    ALU.mult, ALU.add))
                else:
                    k.op("vector", lambda e: e.scalar_tensor_tensor(memC[:], memC[:], 0.9, yC[:],
                                                                    ALU.mult, ALU.add))
                k.op("vector", lambda e: e.tensor_sub(memC[:], memC[:], spkC[:]))
                k.op("vector", lambda e: e.tensor_single_scalar(spkC[:], memC[:], 1.0, ALU.is_gt))
                k.op("vector", lambda e: e.tensor_add(accC[:], accC[:], spkC[:]))
            k.op("vector", lambda e: e.scalar_tensor_tensor(xoC[:], accC[:], 0.2, xoC[:],
                                                            ALU.mult, ALU.add),
                 waits=[(s_xol, xo_ld)], inc=s_dve.inc())
            cst["scan_dve"] = s_dve.n
            cst["acc_dve"] = s_dve.n
            cst["spk_dve"] = s_dve.n
            k.dma("gpsimd", fm_ap(y_d, s, t0, CCH), xoC[:],
                  waits=[(s_dve, s_dve.n)], inc=s_sty.inc(16))

        k.wait("gpsimd", s_sty, s_sty.n)

    k.emit(ctx)
    nc._ctx = ctx
    return nc


_cache = {}


def kernel(**inputs):
    w = prep_weights(inputs)
    x = np.asarray(inputs["L0_attn_x"], np.float32)
    if "prog" not in _cache:
        _cache["prog"] = build_program()
    nc = _cache["prog"]
    in_maps = []
    for c in range(NCORES):
        xs = np.ascontiguousarray(x[c * SEQS:(c + 1) * SEQS].transpose(0, 2, 1))
        m = {"x": xs}
        m.update({kk: vv for kk, vv in w.items() if vv is not None})
        in_maps.append(m)
    res = run_bass_kernel_spmd(nc, in_maps, core_ids=list(range(NCORES)))
    outs = [r["y"].transpose(0, 2, 1) for r in res.results]
    return np.ascontiguousarray(np.concatenate(outs, axis=0))
